# version 05
# baseline (speedup 1.0000x reference)
import contextlib
import numpy as np
import ml_dtypes
import concourse.bass as bass
import concourse.mybir as mybir
from concourse.bass_utils import run_bass_kernel_spmd

F32 = mybir.dt.float32
BF16 = mybir.dt.bfloat16
AF = mybir.ActivationFunctionType
ALU = mybir.AluOpType
AX = mybir.AxisListType

NCORES = 8
DH = 128
QBLK = 128
SPAN = NCORES * QBLK


class Cfg:
    def __init__(self, T=8192, D=4096, DFF=11008, NH=32, NKV=8):
        self.T, self.D, self.DFF, self.NH, self.NKV = T, D, DFF, NH, NKV
        self.TC = T // NCORES
        self.NQB = self.TC // QBLK
        self.DC = D // 128
        self.DKV = NKV * DH
        self.KVC = self.DKV // 128
        self.FFC = DFF // 128
        self.FFH = self.FFC // 2
        self.G = NH // NKV
        self.NT = T // 128
        assert self.TC % QBLK == 0 and self.FFC % 2 == 0 and D % 128 == 0
        assert T % SPAN == 0 and NH % NKV == 0 and self.NT % 4 == 0


class Sem:
    def __init__(self, es, nc, name, step=1):
        self.h = es.enter_context(nc.semaphore(name))
        self.n = 0
        self.step = step


class Emitter:
    ENGINES = ("pe", "dve", "act", "pool", "sp")

    def __init__(self):
        self.ops = {e: [] for e in self.ENGINES}

    def op(self, eng, fn, inc=None, k=1):
        if inc is not None:
            def wrapped(e, fn=fn, s=inc, k=k):
                fn(e).then_inc(s.h, k * s.step)
            self.ops[eng].append(wrapped)
            inc.n += k * inc.step
            return inc.n
        self.ops[eng].append(fn)
        return None

    def wait(self, eng, sem, val):
        if val > 0:
            self.ops[eng].append(lambda e, h=sem.h, v=val: e.wait_ge(h, v))

    def flush(self, nc):
        with nc.Block() as block:
            @block.tensor
            def _(t):
                for f in self.ops["pe"]:
                    f(t)

            @block.vector
            def _(v):
                for f in self.ops["dve"]:
                    f(v)

            @block.scalar
            def _(s):
                for f in self.ops["act"]:
                    f(s)

            @block.gpsimd
            def _(g):
                for f in self.ops["pool"]:
                    f(g)

            @block.sync
            def _(sy):
                for f in self.ops["sp"]:
                    f(sy)


class Dma:

    def __init__(self, es, nc, em, name, queue):
        self.sem = Sem(es, nc, name, step=16)
        self.em = em
        self.q = queue

    def start(self, fn, waits=()):
        em = self.em
        em.wait(self.q, self.sem, self.sem.n)
        for (s, v) in waits:
            em.wait(self.q, s, v)
        em.op(self.q, fn, inc=self.sem)
        return self.sem.n


def build_program(cfg):
    nc = bass.Bass("TRN2", target_bir_lowering=False, debug=False)
    T, D, DFF, NH, NKV = cfg.T, cfg.D, cfg.DFF, cfg.NH, cfg.NKV
    TC, DC, DKV, KVC, FFH, G = (cfg.TC, cfg.DC, cfg.DKV, cfg.KVC,
                                cfg.FFH, cfg.G)
    NQB, NT = cfg.NQB, cfg.NT
    TP = min(TC, 512)
    NTP = TC // TP

    dX = nc.dram_tensor("X", [T, D], F32, kind="ExternalInput").ap()
    dXq = nc.dram_tensor("Xq", [TC, D], F32, kind="ExternalInput").ap()
    dXqT = nc.dram_tensor("XqT", [D, TC], F32, kind="ExternalInput").ap()
    dWq = nc.dram_tensor("Wq", [D, D], F32, kind="ExternalInput").ap()
    dWk = nc.dram_tensor("Wk", [D, DKV], F32, kind="ExternalInput").ap()
    dWv = nc.dram_tensor("Wv", [D, DKV], F32, kind="ExternalInput").ap()
    dWo = nc.dram_tensor("Wo", [D, D], F32, kind="ExternalInput").ap()
    dWgT = nc.dram_tensor("WgT", [D, DFF], F32, kind="ExternalInput").ap()
    dWuT = nc.dram_tensor("WuT", [D, DFF], F32, kind="ExternalInput").ap()
    dWdT = nc.dram_tensor("WdT", [DFF, D], F32, kind="ExternalInput").ap()
    dCkv = nc.dram_tensor("cs_kv", [128, T], F32, kind="ExternalInput").ap()
    dCq = nc.dram_tensor("cs_q", [128, TC], F32, kind="ExternalInput").ap()
    dMask = nc.dram_tensor("maskm", [128, SPAN], BF16, kind="ExternalInput").ap()
    dIdent = nc.dram_tensor("ident", [128, 128], BF16, kind="ExternalInput").ap()
    dOut = nc.dram_tensor("outT", [D, TC], F32, kind="ExternalOutput").ap()

    dKscr = nc.dram_tensor("kscr", [DKV, T], BF16, kind="Internal").ap()
    dVscr = nc.dram_tensor("vscr", [T, DKV], BF16, kind="Internal").ap()
    dFpart = nc.dram_tensor("ffnpart", [D, TC], F32, kind="Internal").ap()
    dFfn = nc.dram_tensor("ffnscr", [D, TC], F32, kind="Internal").ap()
    dAtt = nc.dram_tensor("attscr", [TC, D], BF16, kind="Internal").ap()

    es = contextlib.ExitStack()
    em = Emitter()

    PE = Sem(es, nc, "s_pe")
    DVE = Sem(es, nc, "s_dve")
    ACT = Sem(es, nc, "s_act")
    POOL = Sem(es, nc, "s_pool")

    dmaC = Dma(es, nc, em, "d_const", "sp")
    dmaX = Dma(es, nc, em, "d_x", "pool")
    dmaWKV = Dma(es, nc, em, "d_wkv", "pool")
    dmaWG = Dma(es, nc, em, "d_wg", "pool")
    dmaWU = Dma(es, nc, em, "d_wu", "pool")
    dmaWD = Dma(es, nc, em, "d_wd", "pool")
    dmaKST = Dma(es, nc, em, "d_kst", "sp")
    dmaVST = Dma(es, nc, em, "d_vst", "sp")
    dmaYST = Dma(es, nc, em, "d_yst", "sp")
    dmaFPL = Dma(es, nc, em, "d_fpl", "sp")
    dmaKVL = Dma(es, nc, em, "d_kvl", "sp")
    dmaATT = Dma(es, nc, em, "d_att", "sp")
    dmaXR = Dma(es, nc, em, "d_xr", "sp")
    dmaFR = Dma(es, nc, em, "d_fr", "sp")
    dmaOUT = Dma(es, nc, em, "d_out", "sp")
    all_sems = [PE, DVE, ACT, POOL] + [d.sem for d in (
        dmaC, dmaX, dmaWKV, dmaWG, dmaWU, dmaWD, dmaKST, dmaVST, dmaYST,
        dmaFPL, dmaKVL, dmaATT, dmaXR, dmaFR, dmaOUT)]

    def barrier():
        for eng in em.ENGINES:
            for s in all_sems:
                em.wait(eng, s, s.n)

    pb = [es.enter_context(nc.psum_tensor(f"pb{i}", [128, 512], F32))
          for i in range(8)]

    ident = es.enter_context(nc.sbuf_tensor("ident_sb", [128, 128], BF16))
    maskm = es.enter_context(nc.sbuf_tensor("mask_sb", [128, SPAN], BF16))
    eps_t = es.enter_context(nc.sbuf_tensor("eps_sb", [128, 1], F32))
    st = {
        "bn": es.enter_context(nc.sbuf_tensor("st_bn", [128, 6 * (D // 512)],
                                              F32)),
        "ag": es.enter_context(nc.sbuf_tensor("st_ag", [128, 2], F32)),
        "sq": es.enter_context(nc.sbuf_tensor("st_sq", [128, 1], F32)),
        "rs": es.enter_context(nc.sbuf_tensor("st_rs", [128, 1], F32)),
    }

    dmaC.start(lambda e: e.dma_start(out=ident[:], in_=dIdent))
    const_ld = dmaC.start(lambda e: e.dma_start(out=maskm[:], in_=dMask))
    em.op("pool", lambda e: e.memset(eps_t[:], 1e-5), inc=POOL)
    eps_ready = POOL.n
    em.wait("pe", dmaC.sem, const_ld)
    em.wait("dve", dmaC.sem, const_ld)

    ln_state = {"first": True}

    def ln_tile(xsrc, row0, buf, free_mark):
        dmaX.start(
            lambda e: e.dma_start(out=buf[:], in_=xsrc[row0:row0 + 128, :]),
            waits=[(PE, free_mark)])
        em.wait("dve", dmaX.sem, dmaX.sem.n)
        nblk = D // 512
        for b in range(nblk):
            em.op("dve", lambda e, b=b, buf=buf: e.bn_stats(
                st["bn"][:, 6 * b:6 * b + 6], buf[:, 512 * b:512 * b + 512]),
                inc=DVE)
        em.wait("dve", DVE, DVE.n)
        em.op("dve", lambda e: e.bn_aggr(st["ag"][:], st["bn"][:, 0:6 * nblk]),
              inc=DVE)
        em.wait("act", DVE, DVE.n)
        if ln_state["first"]:
            em.wait("act", POOL, eps_ready)
            ln_state["first"] = False
        em.op("act", lambda e: e.activation(st["sq"][:], st["ag"][:, 1:2],
                                            AF.Sqrt, bias=eps_t[:], scale=1.0),
              inc=ACT)
        em.wait("dve", ACT, ACT.n)
        em.op("dve", lambda e: e.reciprocal(st["rs"][:], st["sq"][:]), inc=DVE)
        em.wait("dve", DVE, DVE.n)
        em.op("dve", lambda e, buf=buf: e.tensor_scalar(
            buf[:], buf[:], st["ag"][:, 0:1], st["rs"][:],
            ALU.subtract, ALU.mult), inc=DVE)
        return DVE.n

    tb_free = {6: 0, 7: 0}

    def transpose_tiles(src, dst, ncols, col0, ln_mark, nch_total):
        em.wait("pe", DVE, ln_mark)
        dview = dst[:].rearrange("p (ch c) -> p ch c", c=ncols)
        for q4 in range((nch_total + 3) // 4):
            bank_i = 6 + q4 % 2
            bank = pb[bank_i]
            nch = min(4, nch_total - 4 * q4)
            em.wait("pe", ACT, tb_free[bank_i])
            for j in range(nch):
                ch = 4 * q4 + j
                em.op("pe", lambda e, ch=ch, bank=bank, j=j, src=src: e.matmul(
                    bank[:, 128 * j:128 * j + 128],
                    src[:, 128 * ch:128 * ch + 128], ident[:],
                    start=True, stop=True), inc=PE)
            em.wait("act", PE, PE.n)
            em.op("act", lambda e, bank=bank, q4=q4, nch=nch, dview=dview:
                  e.copy(dview[:, 4 * q4:4 * q4 + nch, col0:col0 + 128],
                         bank[:].rearrange("p (j c) -> p j c", c=128)[:, 0:nch, :]),
                  inc=ACT)
            tb_free[bank_i] = ACT.n
        return ACT.n, PE.n

    def load_w_slice(dma, dst, wdram, col0, ncols, nch, waits=()):
        return dma.start(lambda e: e.dma_start(
            out=dst[:, 0:nch * ncols].rearrange("p (ic c) -> p ic c", c=ncols),
            in_=wdram[0:nch * 128, col0:col0 + ncols].rearrange(
                "(ic p) c -> p ic c", p=128)), waits=waits)

    def build_xnT(xsrc, ntok, xnT, xbf_bufs, xbf_free):
        m = 0
        for tt in range(ntok // 128):
            bi = tt % 2
            lnm = ln_tile(xsrc, tt * 128, xbf_bufs[bi], xbf_free[bi])
            m, pem = transpose_tiles(xbf_bufs[bi], xnT, ntok, tt * 128,
                                     lnm, DC)
            xbf_free[bi] = pem
        return m

    BS = 1024 if T >= 1024 else 512
    with contextlib.ExitStack() as es1:
        xnT_blk = es1.enter_context(nc.sbuf_tensor("p1xnT", [128, DC * BS],
                                                   BF16))
        xbf = [es1.enter_context(nc.sbuf_tensor(f"xbf{i}", [128, D], BF16))
               for i in range(2)]
        wkv_buf = [es1.enter_context(
            nc.sbuf_tensor(f"wkv{i}", [128, DC * 128], BF16)) for i in range(2)]
        cskv = es1.enter_context(nc.sbuf_tensor("cskv", [128, T], F32))
        krot = [es1.enter_context(nc.sbuf_tensor(f"krot{i}", [128, 512], BF16))
                for i in range(2)]
        kts = [[es1.enter_context(nc.sbuf_tensor(f"kt{j}_{i}", [64, 512], F32))
                for j in range(4)] for i in range(2)]
        vstage = [es1.enter_context(
            nc.sbuf_tensor(f"vst{i}", [128, 512], BF16)) for i in range(2)]
        vstage2 = [es1.enter_context(
            nc.sbuf_tensor(f"vst2_{i}", [128, 512], BF16)) for i in range(2)]

        cskv_ld = dmaC.start(lambda e: e.dma_start(out=cskv[:], in_=dCkv))

        NBLK = T // BS
        NPIECE = BS // 512
        xbf_free = [0, 0]
        wkv_free = [0, 0]
        kbank_free = {0: 0, 2: 0, 4: 0}
        vbank_free = {1: 0, 3: 0, 5: 0}
        krot_free = [0, 0]
        vst_free = [0, 0]
        vst2_free = [0, 0]
        kt_free = [0, 0]
        cskv_waited = {"dve": False}
        vpi = 0

        for blk in range(NBLK):
            for tt in range(BS // 128):
                bi = tt % 2
                lnm = ln_tile(dX, blk * BS + tt * 128, xbf[bi], xbf_free[bi])
                _, pem = transpose_tiles(xbf[bi], xnT_blk, BS, tt * 128,
                                         lnm, DC)
                xbf_free[bi] = pem
            xnT_ready = ACT.n

            for which, wdram in ((0, dWk), (1, dWv)):
                for kc in range(KVC):
                    idx = which * KVC + kc
                    wbi = idx % 2
                    wld = load_w_slice(dmaWKV, wkv_buf[wbi], wdram, kc * 128,
                                       128, DC, waits=[(PE, wkv_free[wbi])])
                    em.wait("pe", dmaWKV.sem, wld)
                    em.wait("pe", ACT, xnT_ready)
                    base = 0 if which == 0 else 1
                    bis = [base + 2 * ((kc * NPIECE + piece) % 3)
                           for piece in range(NPIECE)]
                    banks = [pb[b] for b in bis]
                    for b in bis:
                        em.wait("pe", DVE,
                                (kbank_free if which == 0 else vbank_free)[b])
                    for ic in range(DC):
                        for piece in range(NPIECE):
                            em.op("pe", lambda e, ic=ic, wbi=wbi,
                                  piece=piece, bank=banks[piece]: e.matmul(
                                      bank[:],
                                      wkv_buf[wbi][:, ic * 128:ic * 128 + 128],
                                      xnT_blk[:, ic * BS + piece * 512:
                                              ic * BS + piece * 512 + 512],
                                      start=(ic == 0), stop=(ic == DC - 1)),
                                  inc=PE)
                    wkv_free[wbi] = PE.n
                    for piece in range(NPIECE):
                        bank_i = bis[piece]
                        bank = pb[bank_i]
                        projm = PE.n
                        c0 = blk * BS + piece * 512
                        if which == 0:
                            par = (kc * NPIECE + piece) % 2
                            kb = krot[par]
                            t1, t2, t3, t4 = kts[par]
                            em.wait("dve", PE, projm)
                            if not cskv_waited["dve"]:
                                em.wait("dve", dmaC.sem, cskv_ld)
                                cskv_waited["dve"] = True
                            em.wait("dve", dmaKST.sem, krot_free[par])
                            em.wait("dve", DVE, kt_free[par])
                            for dst, rlo, clo in ((t1, 0, 0), (t2, 64, 64),
                                                  (t3, 0, 64), (t4, 64, 0)):
                                em.op("dve", lambda e, dst=dst, rlo=rlo,
                                      clo=clo, bank=bank, c0=c0:
                                      e.tensor_tensor(
                                          dst[:], bank[rlo:rlo + 64, :],
                                          cskv[clo:clo + 64, c0:c0 + 512],
                                          ALU.mult), inc=DVE)
                            em.wait("dve", DVE, DVE.n)
                            em.op("dve", lambda e, kb=kb, t1=t1, t2=t2:
                                  e.tensor_tensor(kb[0:64, :], t1[:], t2[:],
                                                  ALU.subtract), inc=DVE)
                            em.op("dve", lambda e, kb=kb, t3=t3, t4=t4:
                                  e.tensor_tensor(kb[64:128, :], t3[:], t4[:],
                                                  ALU.add), inc=DVE)
                            kt_free[par] = DVE.n
                            kbank_free[bank_i] = DVE.n
                            krot_free[par] = dmaKST.start(
                                lambda e, kb=kb, kc=kc, c0=c0: e.dma_start(
                                    out=dKscr[kc * 128:kc * 128 + 128,
                                              c0:c0 + 512],
                                    in_=kb[:]), waits=[(DVE, DVE.n)])
                        else:
                            par = vpi % 2
                            vb = vstage[par]
                            vb2 = vstage2[par]
                            em.wait("dve", PE, projm)
                            em.wait("dve", PE, vst_free[par])
                            em.op("dve", lambda e, vb=vb, bank=bank:
                                  e.tensor_copy(vb[:], bank[:]), inc=DVE)
                            vbank_free[bank_i] = DVE.n
                            vevac = DVE.n
                            tbank_i = 6 + vpi % 2
                            tbank = pb[tbank_i]
                            em.wait("pe", DVE, vevac)
                            em.wait("pe", ACT, tb_free[tbank_i])
                            for j in range(4):
                                em.op("pe", lambda e, j=j, vb=vb, tbank=tbank:
                                      e.matmul(
                                          tbank[:, 128 * j:128 * j + 128],
                                          vb[:, 128 * j:128 * j + 128],
                                          ident[:], start=True, stop=True),
                                      inc=PE)
                            vst_free[par] = PE.n
                            em.wait("act", PE, PE.n)
                            em.wait("act", dmaVST.sem, vst2_free[par])
                            em.op("act", lambda e, vb2=vb2, tbank=tbank:
                                  e.copy(vb2[:], tbank[:]), inc=ACT)
                            tb_free[tbank_i] = ACT.n
                            vst2_free[par] = dmaVST.start(
                                lambda e, vb2=vb2, kc=kc, c0=c0: e.dma_start(
                                    out=dVscr[c0:c0 + 512,
                                              kc * 128:kc * 128 + 128]
                                    .rearrange("(cc p) d -> p cc d", p=128),
                                    in_=vb2[:].rearrange(
                                        "p (cc d) -> p cc d", d=128)),
                                waits=[(ACT, ACT.n)])
                            vpi += 1

        p1_kst_done = dmaKST.sem.n
        p1_vst_done = dmaVST.sem.n
        barrier()

    with contextlib.ExitStack() as es2:
        xnT_mine = es2.enter_context(nc.sbuf_tensor("xnTm", [128, DC * TC],
                                                    BF16))
        with contextlib.ExitStack() as es2a:
            xbf2 = [es2a.enter_context(nc.sbuf_tensor(f"x2bf{i}", [128, D],
                                                      BF16)) for i in range(2)]
            xnTm_ready = build_xnT(dXq, TC, xnT_mine, xbf2, [0, 0])
            barrier()

        h_half = es2.enter_context(nc.sbuf_tensor("h_half", [128, FFH * TC],
                                                  BF16))
        ffn_done = 0
        for half in range(2):
            with contextlib.ExitStack() as es2b:
                wg_buf = [es2b.enter_context(
                    nc.sbuf_tensor(f"wg{half}_{i}", [128, DC * 128], BF16))
                    for i in range(2)]
                wu_buf = [es2b.enter_context(
                    nc.sbuf_tensor(f"wu{half}_{i}", [128, DC * 128], BF16))
                    for i in range(2)]
                silu_t = [es2b.enter_context(
                    nc.sbuf_tensor(f"silu{half}_{i}", [128, TP], F32))
                    for i in range(2)]
                wg_free = [PE.n, PE.n]
                wu_free = [PE.n, PE.n]
                silu_free = [0, 0]
                gu_bank_free = [DVE.n] * 4
                for oo in range(FFH):
                    o = half * FFH + oo
                    wbi = oo % 2
                    gld = load_w_slice(dmaWG, wg_buf[wbi], dWgT, o * 128, 128,
                                       DC, waits=[(PE, wg_free[wbi])])
                    uld = load_w_slice(dmaWU, wu_buf[wbi], dWuT, o * 128, 128,
                                       DC, waits=[(PE, wu_free[wbi])])
                    em.wait("pe", dmaWG.sem, gld)
                    em.wait("pe", dmaWU.sem, uld)
                    if oo == 0 and half == 0:
                        em.wait("pe", ACT, xnTm_ready)
                    for b in range(4):
                        em.wait("pe", DVE, gu_bank_free[b])
                    for ic in range(DC):
                        for tp in range(NTP):
                            em.op("pe", lambda e, ic=ic, tp=tp, wbi=wbi,
                                  wg_buf=wg_buf: e.matmul(
                                      pb[tp % 2][:, 0:TP],
                                      wg_buf[wbi][:, ic * 128:ic * 128 + 128],
                                      xnT_mine[:, ic * TC + tp * TP:
                                               ic * TC + tp * TP + TP],
                                      start=(ic == 0), stop=(ic == DC - 1)),
                                  inc=PE)
                        for tp in range(NTP):
                            em.op("pe", lambda e, ic=ic, tp=tp, wbi=wbi,
                                  wu_buf=wu_buf: e.matmul(
                                      pb[2 + tp % 2][:, 0:TP],
                                      wu_buf[wbi][:, ic * 128:ic * 128 + 128],
                                      xnT_mine[:, ic * TC + tp * TP:
                                               ic * TC + tp * TP + TP],
                                      start=(ic == 0), stop=(ic == DC - 1)),
                                  inc=PE)
                    gu_m = PE.n
                    wg_free[wbi] = gu_m
                    wu_free[wbi] = gu_m
                    for tp in range(NTP):
                        stl = silu_t[tp % 2]
                        em.wait("act", PE, gu_m)
                        em.wait("act", DVE, silu_free[tp % 2])
                        em.op("act", lambda e, stl=stl, tp=tp: e.activation(
                            stl[:, 0:TP], pb[tp % 2][:, 0:TP], AF.Silu,
                            bias=0.0, scale=1.0), inc=ACT)
                        em.wait("dve", ACT, ACT.n)
                        em.op("dve", lambda e, stl=stl, tp=tp, oo=oo:
                              e.tensor_tensor(
                                  h_half[:, oo * TC + tp * TP:
                                         oo * TC + tp * TP + TP],
                                  stl[:, 0:TP], pb[2 + tp % 2][:, 0:TP],
                                  ALU.mult), inc=DVE)
                        silu_free[tp % 2] = DVE.n
                        gu_bank_free[tp % 2] = DVE.n
                        gu_bank_free[2 + tp % 2] = DVE.n
                    em.wait("pe", ACT, ACT.n)
                h_ready = DVE.n
                barrier()

            with contextlib.ExitStack() as es2c:
                wd_buf = [es2c.enter_context(
                    nc.sbuf_tensor(f"wd{half}_{i}", [128, FFH * 128], BF16))
                    for i in range(2)]
                ystage = [es2c.enter_context(
                    nc.sbuf_tensor(f"yst{half}_{i}", [128, TC], F32))
                    for i in range(2)]
                fpart_t = [es2c.enter_context(
                    nc.sbuf_tensor(f"fpt{half}_{i}", [128, TC], F32))
                    for i in range(2)]
                wd_free = [PE.n, PE.n]
                ystage_free = [0, 0]
                fp_free = [0, 0]
                dn_bank_free = {4: DVE.n, 5: DVE.n}
                for od in range(DC):
                    wbi = od % 2
                    wld = dmaWD.start(
                        lambda e, wbi=wbi, od=od, half=half, wd_buf=wd_buf:
                        e.dma_start(
                            out=wd_buf[wbi][:].rearrange(
                                "p (fc c) -> p fc c", c=128),
                            in_=dWdT[half * FFH * 128:(half + 1) * FFH * 128,
                                     od * 128:od * 128 + 128].rearrange(
                                         "(fc p) c -> p fc c", p=128)),
                        waits=[(PE, wd_free[wbi])])
                    em.wait("pe", dmaWD.sem, wld)
                    if od == 0:
                        em.wait("pe", DVE, h_ready)
                    for tp in range(NTP):
                        em.wait("pe", DVE, dn_bank_free[4 + tp % 2])
                    for fc in range(FFH):
                        for tp in range(NTP):
                            em.op("pe", lambda e, fc=fc, tp=tp, wbi=wbi,
                                  wd_buf=wd_buf: e.matmul(
                                      pb[4 + tp % 2][:, 0:TP],
                                      wd_buf[wbi][:, fc * 128:fc * 128 + 128],
                                      h_half[:, fc * TC + tp * TP:
                                             fc * TC + tp * TP + TP],
                                      start=(fc == 0), stop=(fc == FFH - 1)),
                                  inc=PE)
                    dn_m = PE.n
                    wd_free[wbi] = dn_m
                    ys = ystage[od % 2]
                    em.wait("dve", PE, dn_m)
                    em.wait("dve", dmaYST.sem, ystage_free[od % 2])
                    if half == 0:
                        for tp in range(NTP):
                            em.op("dve", lambda e, ys=ys, tp=tp: e.tensor_copy(
                                ys[:, tp * TP:tp * TP + TP],
                                pb[4 + tp % 2][:, 0:TP]), inc=DVE)
                        for tp in range(NTP):
                            dn_bank_free[4 + tp % 2] = DVE.n
                        ystage_free[od % 2] = dmaYST.start(
                            lambda e, ys=ys, od=od: e.dma_start(
                                out=dFpart[od * 128:od * 128 + 128, :],
                                in_=ys[:]), waits=[(DVE, DVE.n)])
                    else:
                        fp = fpart_t[od % 2]
                        fld = dmaFPL.start(lambda e, fp=fp, od=od: e.dma_start(
                            out=fp[:], in_=dFpart[od * 128:od * 128 + 128, :]),
                            waits=[(DVE, fp_free[od % 2])])
                        em.wait("dve", dmaFPL.sem, fld)
                        for tp in range(NTP):
                            em.op("dve", lambda e, ys=ys, fp=fp, tp=tp:
                                  e.tensor_tensor(
                                      ys[:, tp * TP:tp * TP + TP],
                                      fp[:, tp * TP:tp * TP + TP],
                                      pb[4 + tp % 2][:, 0:TP], ALU.add),
                                  inc=DVE)
                        for tp in range(NTP):
                            dn_bank_free[4 + tp % 2] = DVE.n
                        fp_free[od % 2] = DVE.n
                        ystage_free[od % 2] = dmaYST.start(
                            lambda e, ys=ys, od=od: e.dma_start(
                                out=dFfn[od * 128:od * 128 + 128, :],
                                in_=ys[:]), waits=[(DVE, DVE.n)])
                if half == 1:
                    ffn_done = dmaYST.sem.n
                barrier()

    es3 = contextlib.ExitStack()
    qrotT = es3.enter_context(nc.sbuf_tensor("qrotT", [128, NH * TC], BF16))
    with contextlib.ExitStack() as es3a:
        xnT3 = es3a.enter_context(nc.sbuf_tensor("xnT3", [128, DC * TC],
                                                 BF16))
        xbf3 = [es3a.enter_context(nc.sbuf_tensor(f"x3bf{i}", [128, D], BF16))
                for i in range(2)]
        xnT3_ready = build_xnT(dXq, TC, xnT3, xbf3, [0, 0])
        csq = es3a.enter_context(nc.sbuf_tensor("csq", [128, TC], F32))
        csq_ld = dmaC.start(lambda e: e.dma_start(out=csq[:], in_=dCq))
        wq_buf = [es3a.enter_context(
            nc.sbuf_tensor(f"wq{i}", [128, DC * 128], BF16)) for i in range(2)]
        qts = [[es3a.enter_context(
            nc.sbuf_tensor(f"qt{j}_{i}", [64, TP], F32)) for j in range(4)]
            for i in range(2)]
        qt_free = [0, 0]
        wq_free = [PE.n, PE.n]
        qbank_free = {0: DVE.n, 1: DVE.n, 2: DVE.n, 3: DVE.n}
        csq_waited = {"dve": False}
        for hh in range(NH):
            wbi = hh % 2
            wld = load_w_slice(dmaWG, wq_buf[wbi], dWq, hh * 128, 128, DC,
                               waits=[(PE, wq_free[wbi])])
            em.wait("pe", dmaWG.sem, wld)
            if hh == 0:
                em.wait("pe", ACT, xnT3_ready)
            qb0 = 2 * (hh % 2)
            for tp in range(NTP):
                em.wait("pe", DVE, qbank_free[qb0 + tp % 2])
            for ic in range(DC):
                for tp in range(NTP):
                    em.op("pe", lambda e, ic=ic, tp=tp, wbi=wbi, qb0=qb0:
                          e.matmul(
                              pb[qb0 + tp % 2][:, 0:TP],
                              wq_buf[wbi][:, ic * 128:ic * 128 + 128],
                              xnT3[:, ic * TC + tp * TP:
                                   ic * TC + tp * TP + TP],
                              start=(ic == 0), stop=(ic == DC - 1)), inc=PE)
            wq_free[wbi] = PE.n
            for tp in range(NTP):
                bank_i = qb0 + tp % 2
                bank = pb[bank_i]
                t1, t2, t3, t4 = qts[tp % 2]
                em.wait("dve", PE, PE.n)
                if not csq_waited["dve"]:
                    em.wait("dve", dmaC.sem, csq_ld)
                    csq_waited["dve"] = True
                em.wait("dve", DVE, qt_free[tp % 2])
                c0 = tp * TP
                qcol = hh * TC + tp * TP
                for dst, rlo, clo in ((t1, 0, 0), (t2, 64, 64),
                                      (t3, 0, 64), (t4, 64, 0)):
                    em.op("dve", lambda e, dst=dst, rlo=rlo, clo=clo,
                          bank=bank, c0=c0: e.tensor_tensor(
                              dst[:, 0:TP], bank[rlo:rlo + 64, 0:TP],
                              csq[clo:clo + 64, c0:c0 + TP], ALU.mult),
                          inc=DVE)
                em.wait("dve", DVE, DVE.n)
                em.op("dve", lambda e, t1=t1, t2=t2, qcol=qcol:
                      e.tensor_tensor(qrotT[0:64, qcol:qcol + TP],
                                      t1[:, 0:TP], t2[:, 0:TP],
                                      ALU.subtract), inc=DVE)
                em.op("dve", lambda e, t3=t3, t4=t4, qcol=qcol:
                      e.tensor_tensor(qrotT[64:128, qcol:qcol + TP],
                                      t3[:, 0:TP], t4[:, 0:TP], ALU.add),
                      inc=DVE)
                qt_free[tp % 2] = DVE.n
                qbank_free[bank_i] = DVE.n
        qrot_ready = DVE.n
        barrier()

    with contextlib.ExitStack() as es4:
        ksb = es4.enter_context(nc.sbuf_tensor("ksb", [128, T], BF16))
        vsb = es4.enter_context(nc.sbuf_tensor("vsb", [128, T], BF16))
        sco2 = [es4.enter_context(nc.sbuf_tensor(f"sco{i}", [128, T], F32))
                for i in range(2)]
        psb2 = [es4.enter_context(nc.sbuf_tensor(f"psb{i}", [128, T], BF16))
                for i in range(2)]
        pt_sb = [es4.enter_context(nc.sbuf_tensor(f"pt{i}", [128, 512], BF16))
                 for i in range(2)]
        attst = [es4.enter_context(nc.sbuf_tensor(f"atts{i}", [128, G * 128],
                                                  BF16)) for i in range(2)]
        mx2 = [es4.enter_context(nc.sbuf_tensor(f"mx{i}", [128, 16], F32))
               for i in range(2)]
        negm2 = [es4.enter_context(nc.sbuf_tensor(f"ngm{i}", [128, 1], F32))
                 for i in range(2)]
        ssum2 = [es4.enter_context(nc.sbuf_tensor(f"ssm{i}", [128, 1], F32))
                 for i in range(2)]
        rs2 = [es4.enter_context(nc.sbuf_tensor(f"rs{i}", [128, 1], F32))
               for i in range(2)]
        ssp2 = [es4.enter_context(nc.sbuf_tensor(f"ssp{i}", [128, 4], F32))
                for i in range(2)]

        sc_free = {i: DVE.n for i in range(4)}
        ptb_free = {4: (DVE, DVE.n), 5: (DVE, DVE.n)}
        av_free = {6: DVE.n, 7: DVE.n}
        pt_free = [0, 0]
        psb_free = [0, 0]
        sco_free = [0, 0]
        att_free_t = [0, 0]
        exp_piece_marks = {}
        rs_marks = {}
        negm_marks = {}
        stage_info = {}

        def qk_emitters(i, t, g, h):
            pq = i % 2
            sco = sco2[pq]
            nk512 = 2 * (t + 1)
            q0 = (h * G + g) * TC + t * 128

            def one(ks):
                bank_i = ks % 4
                bank = pb[bank_i]
                em.wait("pe", DVE, sc_free[bank_i])
                diag = (ks >= nk512 - 2)
                em.op("pe", lambda e, ks=ks, bank=bank, q0=q0, diag=diag:
                      e.matmul(bank[:], qrotT[:, q0:q0 + 128],
                               ksb[:, ks * 512:ks * 512 + 512],
                               start=True, stop=not diag), inc=PE)
                if diag:
                    mof = (ks - (nk512 - 2)) * 512
                    em.op("pe", lambda e, bank=bank, mof=mof:
                          e.matmul(bank[:], ident[:], maskm[:, mof:mof + 512],
                                   start=False, stop=True), inc=PE)
                em.wait("dve", PE, PE.n)
                em.op("dve", lambda e, ks=ks, bank=bank, sco=sco, pq=pq:
                      e.tensor_scalar(
                          sco[:, ks * 512:ks * 512 + 512], bank[:],
                          0.0, None, ALU.add, ALU.max,
                          accum_out=mx2[pq][:, ks:ks + 1]), inc=DVE)
                sc_free[bank_i] = DVE.n
            return [lambda ks=ks: one(ks) for ks in range(nk512)]

        def qk_finish(i, t):
            pq = i % 2
            nk512 = 2 * (t + 1)
            em.wait("dve", DVE, DVE.n)
            em.op("dve", lambda e, nk512=nk512, pq=pq: e.tensor_reduce(
                negm2[pq][:], mx2[pq][:, 0:nk512], AX.X, ALU.max,
                negate=True), inc=DVE)
            negm_marks[i] = DVE.n

        def emit_sc2(i):
            t, g, h = stage_info[i]
            pq = i % 2
            np_ = 256 * (t + 1)
            em.wait("act", DVE, negm_marks[i])
            em.wait("act", PE, psb_free[pq])
            marks = []
            for p in range(4):
                em.op("act", lambda e, pq=pq, p=p, np_=np_: e.activation(
                    psb2[pq][:, p * np_:(p + 1) * np_],
                    sco2[pq][:, p * np_:(p + 1) * np_],
                    AF.Exp, bias=negm2[pq][:], scale=1.0,
                    accum_out=ssp2[pq][:, p:p + 1]), inc=ACT)
                marks.append(ACT.n)
            exp_piece_marks[i] = marks
            sco_free[pq] = marks[-1]
            em.wait("dve", ACT, marks[-1])
            em.op("dve", lambda e, pq=pq: e.reduce_sum(
                ssum2[pq][:], ssp2[pq][:], AX.X), inc=DVE)
            em.wait("dve", DVE, DVE.n)
            em.op("dve", lambda e, pq=pq: e.reciprocal(rs2[pq][:],
                                                       ssum2[pq][:]), inc=DVE)
            rs_marks[i] = DVE.n

        class Trav:

            def __init__(self, i):
                self.i = i
                self.t, self.g, self.h = stage_info[i]
                self.pq = i % 2
                self.ncc = 8 * (self.t + 1)
                self.nb = self.ncc // 4
                self.avb_i = 6 + i % 2
                self.avbank = pb[self.avb_i]
                self.ptev_marks = [0, 0]
                self.av_waited = False

            def do_av(self, c4):
                pt = pt_sb[c4 % 2]
                esem = DVE if c4 % 2 == 0 else ACT
                em.wait("pe", esem, self.ptev_marks[c4 % 2])
                if not self.av_waited:
                    em.wait("pe", DVE, av_free[self.avb_i])
                    self.av_waited = True
                for j in range(4):
                    cc = c4 * 4 + j
                    em.op("pe", lambda e, cc=cc, pt=pt, s=self: e.matmul(
                        s.avbank[:, 0:128],
                        pt[:, 128 * (cc % 4):128 * (cc % 4) + 128],
                        vsb[:, 128 * cc:128 * cc + 128],
                        start=(cc == 0), stop=(cc == s.ncc - 1)), inc=PE)
                pt_free[c4 % 2] = PE.n

            def batch(self, c4):
                bank_i = 4 + c4 % 2
                bank = pb[bank_i]
                feng, fval = ptb_free[bank_i]
                em.wait("pe", feng, fval)
                pneed = -(-2 * (c4 + 1) // (self.t + 1)) - 1
                em.wait("pe", ACT, exp_piece_marks[self.i][pneed])
                for j in range(4):
                    cc = c4 * 4 + j
                    em.op("pe", lambda e, cc=cc, bank=bank, j=j, s=self:
                          e.matmul(
                              bank[:, 128 * j:128 * j + 128],
                              psb2[s.pq][:, 128 * cc:128 * cc + 128],
                              ident[:], start=True, stop=True), inc=PE)
                tr_m = PE.n
                if c4 > 0:
                    self.do_av(c4 - 1)
                pt = pt_sb[c4 % 2]
                eng = "dve" if c4 % 2 == 0 else "act"
                esem = DVE if c4 % 2 == 0 else ACT
                em.wait(eng, PE, tr_m)
                em.wait(eng, PE, pt_free[c4 % 2])
                if eng == "dve":
                    em.op("dve", lambda e, pt=pt, bank=bank: e.tensor_copy(
                        pt[:], bank[:]), inc=DVE)
                else:
                    em.op("act", lambda e, pt=pt, bank=bank: e.copy(
                        pt[:], bank[:]), inc=ACT)
                ptb_free[bank_i] = (esem, esem.n)
                self.ptev_marks[c4 % 2] = esem.n

            def finish(self):
                self.do_av(self.nb - 1)
                psb_free[self.pq] = PE.n
                em.wait("dve", PE, PE.n)
                em.wait("dve", DVE, rs_marks[self.i])
                em.wait("dve", dmaATT.sem, att_free_t[self.t % 2])
                t, g, h = self.t, self.g, self.h
                em.op("dve", lambda e, g=g, t=t, s=self, pq=self.pq:
                      e.tensor_scalar(
                          attst[t % 2][:, g * 128:g * 128 + 128],
                          s.avbank[:, 0:128], rs2[pq][:], None, ALU.mult),
                      inc=DVE)
                av_free[self.avb_i] = DVE.n
                if g == G - 1:
                    att_free_t[t % 2] = dmaATT.start(
                        lambda e, h=h, t=t: e.dma_start(
                            out=dAtt[t * 128:t * 128 + 128,
                                     h * G * 128:(h + 1) * G * 128],
                            in_=attst[t % 2][:]),
                        waits=[(DVE, DVE.n)])

        def emit_pair(i, t, g, h, prev):
            pq = i % 2
            em.wait("dve", ACT, sco_free[pq])
            qks = qk_emitters(i, t, g, h)
            tr = Trav(prev) if prev is not None else None
            lead = t + 1
            for k in range(lead):
                qks[k]()
            bi_ = 0
            for k in range(lead, len(qks)):
                qks[k]()
                for _ in range(2):
                    if tr is not None and bi_ < tr.nb:
                        tr.batch(bi_)
                        bi_ += 1
            while tr is not None and bi_ < tr.nb:
                tr.batch(bi_)
                bi_ += 1
            qk_finish(i, t)
            if tr is not None:
                tr.finish()
            stage_info[i] = (t, g, h)

        prev = None
        for h in range(NKV):
            if prev is not None:
                tr = Trav(prev)
                for c4 in range(tr.nb):
                    tr.batch(c4)
                tr.finish()
                prev = None
            kld = dmaKVL.start(lambda e, h=h: e.dma_start(
                out=ksb[:], in_=dKscr[h * 128:h * 128 + 128, :]),
                waits=([(dmaKST.sem, p1_kst_done), (PE, PE.n)] if h == 0
                       else [(PE, PE.n)]))
            vld = dmaKVL.start(lambda e, h=h: e.dma_start(
                out=vsb[:].rearrange("p (cc d) -> p cc d", d=128),
                in_=dVscr[:, h * 128:h * 128 + 128].rearrange(
                    "(cc p) d -> p cc d", p=128)),
                waits=([(dmaVST.sem, p1_vst_done)] if h == 0 else []))
            em.wait("pe", dmaKVL.sem, vld)
            if h == 0:
                em.wait("pe", DVE, qrot_ready)
            for t in range(NQB):
                for g in range(G):
                    i = (h * NQB + t) * G + g
                    stage_info[i] = (t, g, h)
                    emit_pair(i, t, g, h, prev)
                    emit_sc2(i)
                    prev = i
        tr = Trav(prev)
        for c4 in range(tr.nb):
            tr.batch(c4)
        tr.finish()
        barrier()
    es3.close()
    barrier()

    with contextlib.ExitStack() as es5:
        attT = es5.enter_context(nc.sbuf_tensor("attT", [128, DC * TC], BF16))
        wo_buf = [es5.enter_context(
            nc.sbuf_tensor(f"wo{i}", [128, DC * 128], BF16)) for i in range(2)]
        xr_t = [es5.enter_context(nc.sbuf_tensor(f"xr{i}", [128, TC], F32))
                for i in range(2)]
        fr_t = [es5.enter_context(nc.sbuf_tensor(f"fr{i}", [128, TC], F32))
                for i in range(2)]
        o1_t = [es5.enter_context(nc.sbuf_tensor(f"o1{i}", [128, TC], F32))
                for i in range(2)]

        att_tile = [es5.enter_context(nc.sbuf_tensor(f"at5_{i}", [128, D],
                                                     BF16)) for i in range(2)]
        at_free = [PE.n, PE.n]
        attT_ready = 0
        for tt in range(TC // 128):
            bi = tt % 2
            ld = dmaATT.start(lambda e, tt=tt, bi=bi: e.dma_start(
                out=att_tile[bi][:], in_=dAtt[tt * 128:tt * 128 + 128, :]),
                waits=[(PE, at_free[bi])])
            em.wait("pe", dmaATT.sem, ld)
            attT_ready, pem = transpose_tiles(att_tile[bi], attT, TC,
                                              tt * 128, 0, DC)
            at_free[bi] = pem
        wo_free = [PE.n, PE.n]
        obank_free = {0: DVE.n, 1: DVE.n}
        res_marks = [0, 0]
        out_marks = [0, 0]
        for od in range(DC):
            wbi = od % 2
            wld = load_w_slice(dmaWG, wo_buf[wbi], dWo, od * 128, 128, DC,
                               waits=[(PE, wo_free[wbi])])
            xr, fr, o1 = xr_t[od % 2], fr_t[od % 2], o1_t[od % 2]
            xld = dmaXR.start(lambda e, xr=xr, od=od: e.dma_start(
                out=xr[:], in_=dXqT[od * 128:od * 128 + 128, :]),
                waits=[(DVE, res_marks[od % 2])])
            fld = dmaFR.start(lambda e, fr=fr, od=od: e.dma_start(
                out=fr[:], in_=dFfn[od * 128:od * 128 + 128, :]),
                waits=([(dmaYST.sem, ffn_done), (DVE, res_marks[od % 2])]
                       if od == 0 else [(DVE, res_marks[od % 2])]))
            em.wait("pe", dmaWG.sem, wld)
            if od == 0:
                em.wait("pe", ACT, attT_ready)
            for tp in range(NTP):
                em.wait("pe", DVE, obank_free[tp % 2])
            for ic in range(DC):
                for tp in range(NTP):
                    em.op("pe", lambda e, ic=ic, tp=tp, wbi=wbi: e.matmul(
                        pb[tp % 2][:, 0:TP],
                        wo_buf[wbi][:, ic * 128:ic * 128 + 128],
                        attT[:, ic * TC + tp * TP: ic * TC + tp * TP + TP],
                        start=(ic == 0), stop=(ic == DC - 1)), inc=PE)
            wo_free[wbi] = PE.n
            em.wait("dve", PE, PE.n)
            em.wait("dve", dmaXR.sem, xld)
            em.wait("dve", dmaFR.sem, fld)
            em.wait("dve", dmaOUT.sem, out_marks[od % 2])
            for tp in range(NTP):
                em.op("dve", lambda e, o1=o1, xr=xr, tp=tp: e.tensor_tensor(
                    o1[:, tp * TP:tp * TP + TP], xr[:, tp * TP:tp * TP + TP],
                    pb[tp % 2][:, 0:TP], ALU.add), inc=DVE)
                obank_free[tp % 2] = DVE.n
            em.wait("dve", DVE, DVE.n)
            em.op("dve", lambda e, o1=o1, fr=fr: e.tensor_tensor(
                o1[:], o1[:], fr[:], ALU.add), inc=DVE)
            res_marks[od % 2] = DVE.n
            out_marks[od % 2] = dmaOUT.start(
                lambda e, o1=o1, od=od: e.dma_start(
                    out=dOut[od * 128:od * 128 + 128, :], in_=o1[:]),
                waits=[(DVE, DVE.n)])
        em.wait("sp", dmaOUT.sem, dmaOUT.sem.n)

    em.flush(nc)
    es.close()
    return nc



def _host_inputs(cfg, core, X, Wq, Wk, Wv, Wo, Wg, Wu, Wd):
    T = cfg.T
    f32 = np.float32
    bf16 = ml_dtypes.bfloat16
    Xq = np.ascontiguousarray(X[core::NCORES])
    XqT = np.ascontiguousarray(Xq.T)

    half = DH // 2
    inv = 1.0 / (10000.0 ** (np.arange(half, dtype=np.float64) / half))
    pos_all = np.arange(T, dtype=np.float64)
    ang = pos_all[None, :] * inv[:, None]
    cs_kv = np.concatenate([np.cos(ang), np.sin(ang)], axis=0).astype(f32)
    pos_q = np.arange(T, dtype=np.float64)[core::NCORES]
    angq = pos_q[None, :] * inv[:, None]
    s = 1.0 / np.sqrt(np.float64(DH))
    cs_q = np.concatenate([np.cos(angq) * s, np.sin(angq) * s],
                          axis=0).astype(f32)

    mp = np.arange(SPAN)[None, :] <= (NCORES * np.arange(QBLK)[:, None] + core)
    maskm = np.where(mp, 0.0, -30000.0).astype(f32).astype(bf16)
    ident = np.eye(128, dtype=f32).astype(bf16)

    return {
        "X": X, "Xq": Xq, "XqT": XqT,
        "Wq": Wq, "Wk": Wk, "Wv": Wv, "Wo": Wo,
        "WgT": np.ascontiguousarray(Wg.T),
        "WuT": np.ascontiguousarray(Wu.T),
        "WdT": np.ascontiguousarray(Wd.T),
        "cs_kv": cs_kv, "cs_q": cs_q, "maskm": maskm, "ident": ident,
    }


def run(cfg, X, Wq, Wk, Wv, Wo, Wg, Wu, Wd, trace=False):
    nc = build_program(cfg)
    in_maps = [_host_inputs(cfg, c, X, Wq, Wk, Wv, Wo, Wg, Wu, Wd)
               for c in range(NCORES)]
    res = run_bass_kernel_spmd(nc, in_maps, core_ids=list(range(NCORES)),
                               trace=trace)
    out = np.empty((cfg.T, cfg.D), np.float32)
    for c in range(NCORES):
        out[c::NCORES, :] = res.results[c]["outT"].T
    return out, res


def kernel(L0_attn_X, L0_attn_Wq, L0_attn_Wk, L0_attn_Wv, L0_attn_Wo,
           L0_ffn_W_gate, L0_ffn_W_up, L0_ffn_W_down):
    cfg = Cfg()
    X = np.ascontiguousarray(np.asarray(L0_attn_X, np.float32).reshape(
        cfg.T, cfg.D))
    out, _ = run(cfg, X,
                 np.asarray(L0_attn_Wq, np.float32),
                 np.asarray(L0_attn_Wk, np.float32),
                 np.asarray(L0_attn_Wv, np.float32),
                 np.asarray(L0_attn_Wo, np.float32),
                 np.asarray(L0_ffn_W_gate, np.float32),
                 np.asarray(L0_ffn_W_up, np.float32),
                 np.asarray(L0_ffn_W_down, np.float32))
    return out.reshape(1, cfg.T, cfg.D)


# version 08
# speedup vs baseline: 1.3306x; 1.3306x over previous
import contextlib
import numpy as np
import ml_dtypes
import concourse.bass as bass
import concourse.mybir as mybir
from concourse.bass_utils import run_bass_kernel_spmd

F32 = mybir.dt.float32
BF16 = mybir.dt.bfloat16
AF = mybir.ActivationFunctionType
ALU = mybir.AluOpType
AX = mybir.AxisListType

NCORES = 8
DH = 128
QBLK = 128
SPAN = NCORES * QBLK


class Cfg:
    def __init__(self, T=8192, D=4096, DFF=11008, NH=32, NKV=8):
        self.T, self.D, self.DFF, self.NH, self.NKV = T, D, DFF, NH, NKV
        self.TC = T // NCORES
        self.NQB = self.TC // QBLK
        self.DC = D // 128
        self.DKV = NKV * DH
        self.KVC = self.DKV // 128
        self.FFC = DFF // 128
        self.FFH = self.FFC // 2
        self.G = NH // NKV
        self.NT = T // 128
        assert self.TC % QBLK == 0 and self.FFC % 2 == 0 and D % 128 == 0
        assert T % SPAN == 0 and NH % NKV == 0 and self.NT % 4 == 0


class Sem:
    def __init__(self, es, nc, name, step=1):
        self.h = es.enter_context(nc.semaphore(name))
        self.n = 0
        self.step = step


class Emitter:
    ENGINES = ("pe", "dve", "act", "pool", "sp")

    def __init__(self):
        self.ops = {e: [] for e in self.ENGINES}

    def op(self, eng, fn, inc=None, k=1):
        if inc is not None:
            def wrapped(e, fn=fn, s=inc, k=k):
                fn(e).then_inc(s.h, k * s.step)
            self.ops[eng].append(wrapped)
            inc.n += k * inc.step
            return inc.n
        self.ops[eng].append(fn)
        return None

    def wait(self, eng, sem, val):
        if val > 0:
            self.ops[eng].append(lambda e, h=sem.h, v=val: e.wait_ge(h, v))

    def flush(self, nc):
        with nc.Block() as block:
            @block.tensor
            def _(t):
                for f in self.ops["pe"]:
                    f(t)

            @block.vector
            def _(v):
                for f in self.ops["dve"]:
                    f(v)

            @block.scalar
            def _(s):
                for f in self.ops["act"]:
                    f(s)

            @block.gpsimd
            def _(g):
                for f in self.ops["pool"]:
                    f(g)

            @block.sync
            def _(sy):
                for f in self.ops["sp"]:
                    f(sy)


class Dma:

    def __init__(self, es, nc, em, name, queue):
        self.sem = Sem(es, nc, name, step=16)
        self.em = em
        self.q = queue

    def start(self, fn, waits=()):
        em = self.em
        em.wait(self.q, self.sem, self.sem.n)
        for (s, v) in waits:
            em.wait(self.q, s, v)
        em.op(self.q, fn, inc=self.sem)
        return self.sem.n


def build_program(cfg):
    nc = bass.Bass("TRN2", target_bir_lowering=False, debug=False)
    T, D, DFF, NH, NKV = cfg.T, cfg.D, cfg.DFF, cfg.NH, cfg.NKV
    TC, DC, DKV, KVC, FFH, G = (cfg.TC, cfg.DC, cfg.DKV, cfg.KVC,
                                cfg.FFH, cfg.G)
    NQB, NT = cfg.NQB, cfg.NT
    TP = min(TC, 512)
    NTP = TC // TP

    dXkv = nc.dram_tensor("Xkv", [SPAN, D], F32, kind="ExternalInput").ap()
    dXq = nc.dram_tensor("Xq", [TC, D], F32, kind="ExternalInput").ap()
    dXqT = nc.dram_tensor("XqT", [D, TC], F32, kind="ExternalInput").ap()
    dWq = nc.dram_tensor("Wq", [D, D], F32, kind="ExternalInput").ap()
    dWk = nc.dram_tensor("Wk", [D, DKV], F32, kind="ExternalInput").ap()
    dWv = nc.dram_tensor("Wv", [D, DKV], F32, kind="ExternalInput").ap()
    dWo = nc.dram_tensor("Wo", [D, D], F32, kind="ExternalInput").ap()
    dWgT = nc.dram_tensor("WgT", [D, DFF], F32, kind="ExternalInput").ap()
    dWuT = nc.dram_tensor("WuT", [D, DFF], F32, kind="ExternalInput").ap()
    dWdT = nc.dram_tensor("WdT", [DFF, D], F32, kind="ExternalInput").ap()
    dCkv = nc.dram_tensor("cs_kv", [128, SPAN], F32, kind="ExternalInput").ap()
    dCq = nc.dram_tensor("cs_q", [128, TC], F32, kind="ExternalInput").ap()
    dMask = nc.dram_tensor("maskm", [128, SPAN], BF16, kind="ExternalInput").ap()
    dIdent = nc.dram_tensor("ident", [128, 128], BF16, kind="ExternalInput").ap()
    dOut = nc.dram_tensor("outT", [D, TC], F32, kind="ExternalOutput").ap()

    dContribT = nc.dram_tensor("kvcontrib", [DKV + SPAN, SPAN], BF16,
                               kind="Internal")
    dGathT = nc.dram_tensor("kvgath", [NCORES * (DKV + SPAN), SPAN], BF16,
                            kind="Internal")
    dContrib = dContribT.ap()
    dGath = dGathT.ap()
    dFpart = nc.dram_tensor("ffnpart", [D, TC], F32, kind="Internal").ap()
    dFfn = nc.dram_tensor("ffnscr", [D, TC], F32, kind="Internal").ap()
    dAtt = nc.dram_tensor("attscr", [TC, D], BF16, kind="Internal").ap()

    es = contextlib.ExitStack()
    em = Emitter()

    PE = Sem(es, nc, "s_pe")
    DVE = Sem(es, nc, "s_dve")
    ACT = Sem(es, nc, "s_act")
    POOL = Sem(es, nc, "s_pool")
    CC = Sem(es, nc, "s_cc", step=1)

    dmaC = Dma(es, nc, em, "d_const", "sp")
    dmaX = Dma(es, nc, em, "d_x", "pool")
    dmaWKV = Dma(es, nc, em, "d_wkv", "pool")
    dmaWG = Dma(es, nc, em, "d_wg", "pool")
    dmaWU = Dma(es, nc, em, "d_wu", "pool")
    dmaWD = Dma(es, nc, em, "d_wd", "pool")
    dmaKST = Dma(es, nc, em, "d_kst", "sp")
    dmaVST = Dma(es, nc, em, "d_vst", "sp")
    dmaYST = Dma(es, nc, em, "d_yst", "sp")
    dmaFPL = Dma(es, nc, em, "d_fpl", "sp")
    dmaKVL = Dma(es, nc, em, "d_kvl", "sp")
    dmaATT = Dma(es, nc, em, "d_att", "sp")
    dmaXR = Dma(es, nc, em, "d_xr", "sp")
    dmaFR = Dma(es, nc, em, "d_fr", "sp")
    dmaOUT = Dma(es, nc, em, "d_out", "sp")
    all_sems = [PE, DVE, ACT, POOL] + [d.sem for d in (
        dmaC, dmaX, dmaWKV, dmaWG, dmaWU, dmaWD, dmaKST, dmaVST, dmaYST,
        dmaFPL, dmaKVL, dmaATT, dmaXR, dmaFR, dmaOUT)]

    def barrier():
        for eng in em.ENGINES:
            for s in all_sems:
                em.wait(eng, s, s.n)

    pb = [es.enter_context(nc.psum_tensor(f"pb{i}", [128, 512], F32))
          for i in range(8)]

    ident = es.enter_context(nc.sbuf_tensor("ident_sb", [128, 128], BF16))
    maskm = es.enter_context(nc.sbuf_tensor("mask_sb", [128, SPAN], BF16))
    eps_t = es.enter_context(nc.sbuf_tensor("eps_sb", [128, 1], F32))
    st = {
        "bn": es.enter_context(nc.sbuf_tensor("st_bn", [128, 6 * (D // 512)],
                                              F32)),
        "ag": es.enter_context(nc.sbuf_tensor("st_ag", [128, 2], F32)),
        "sq": es.enter_context(nc.sbuf_tensor("st_sq", [128, 1], F32)),
        "rs": es.enter_context(nc.sbuf_tensor("st_rs", [128, 1], F32)),
    }

    dmaC.start(lambda e: e.dma_start(out=ident[:], in_=dIdent))
    const_ld = dmaC.start(lambda e: e.dma_start(out=maskm[:], in_=dMask))
    em.op("pool", lambda e: e.memset(eps_t[:], 1e-5), inc=POOL)
    eps_ready = POOL.n
    em.wait("pe", dmaC.sem, const_ld)
    em.wait("dve", dmaC.sem, const_ld)

    ln_state = {"first": True}

    def ln_tile(xsrc, row0, buf, free_mark):
        dmaX.start(
            lambda e: e.dma_start(out=buf[:], in_=xsrc[row0:row0 + 128, :]),
            waits=[(PE, free_mark)])
        em.wait("dve", dmaX.sem, dmaX.sem.n)
        nblk = D // 512
        for b in range(nblk):
            em.op("dve", lambda e, b=b, buf=buf: e.bn_stats(
                st["bn"][:, 6 * b:6 * b + 6], buf[:, 512 * b:512 * b + 512]),
                inc=DVE)
        em.wait("dve", DVE, DVE.n)
        em.op("dve", lambda e: e.bn_aggr(st["ag"][:], st["bn"][:, 0:6 * nblk]),
              inc=DVE)
        em.wait("act", DVE, DVE.n)
        if ln_state["first"]:
            em.wait("act", POOL, eps_ready)
            ln_state["first"] = False
        em.op("act", lambda e: e.activation(st["sq"][:], st["ag"][:, 1:2],
                                            AF.Sqrt, bias=eps_t[:], scale=1.0),
              inc=ACT)
        em.wait("dve", ACT, ACT.n)
        em.op("dve", lambda e: e.reciprocal(st["rs"][:], st["sq"][:]), inc=DVE)
        em.wait("dve", DVE, DVE.n)
        em.op("dve", lambda e, buf=buf: e.tensor_scalar(
            buf[:], buf[:], st["ag"][:, 0:1], st["rs"][:],
            ALU.subtract, ALU.mult), inc=DVE)
        return DVE.n

    tb_free = {6: 0, 7: 0}

    def transpose_tiles(src, dst, ncols, col0, ln_mark, nch_total):
        em.wait("pe", DVE, ln_mark)
        dview = dst[:].rearrange("p (ch c) -> p ch c", c=ncols)
        for q4 in range((nch_total + 3) // 4):
            bank_i = 6 + q4 % 2
            bank = pb[bank_i]
            nch = min(4, nch_total - 4 * q4)
            em.wait("pe", ACT, tb_free[bank_i])
            for j in range(nch):
                ch = 4 * q4 + j
                em.op("pe", lambda e, ch=ch, bank=bank, j=j, src=src: e.matmul(
                    bank[:, 128 * j:128 * j + 128],
                    src[:, 128 * ch:128 * ch + 128], ident[:],
                    start=True, stop=True), inc=PE)
            em.wait("act", PE, PE.n)
            em.op("act", lambda e, bank=bank, q4=q4, nch=nch, dview=dview:
                  e.copy(dview[:, 4 * q4:4 * q4 + nch, col0:col0 + 128],
                         bank[:].rearrange("p (j c) -> p j c", c=128)[:, 0:nch, :]),
                  inc=ACT)
            tb_free[bank_i] = ACT.n
        return ACT.n, PE.n

    def load_w_slice(dma, dst, wdram, col0, ncols, nch, waits=()):
        return dma.start(lambda e: e.dma_start(
            out=dst[:, 0:nch * ncols].rearrange("p (ic c) -> p ic c", c=ncols),
            in_=wdram[0:nch * 128, col0:col0 + ncols].rearrange(
                "(ic p) c -> p ic c", p=128)), waits=waits)

    def build_xnT(xsrc, ntok, xnT, xbf_bufs, xbf_free):
        m = 0
        for tt in range(ntok // 128):
            bi = tt % 2
            lnm = ln_tile(xsrc, tt * 128, xbf_bufs[bi], xbf_free[bi])
            m, pem = transpose_tiles(xbf_bufs[bi], xnT, ntok, tt * 128,
                                     lnm, DC)
            xbf_free[bi] = pem
        return m

    BS = SPAN
    with contextlib.ExitStack() as es1:
        xnT2 = [es1.enter_context(nc.sbuf_tensor(f"p1xnT{i}", [128, DC * BS],
                                                 BF16)) for i in range(2)]
        xbf = [es1.enter_context(nc.sbuf_tensor(f"xbf{i}", [128, D], BF16))
               for i in range(2)]
        wkv_buf = [es1.enter_context(
            nc.sbuf_tensor(f"wkv{i}", [128, DC * 128], BF16)) for i in range(2)]
        csb = [es1.enter_context(nc.sbuf_tensor(f"csb{i}", [128, BS], F32))
               for i in range(2)]
        krot = [es1.enter_context(nc.sbuf_tensor(f"krot{i}", [128, 512], BF16))
                for i in range(2)]
        kts = [[es1.enter_context(nc.sbuf_tensor(f"kt{j}_{i}", [64, 512], F32))
                for j in range(4)] for i in range(2)]
        vstage = [es1.enter_context(
            nc.sbuf_tensor(f"vst{i}", [128, 512], BF16)) for i in range(2)]
        vstage2 = [es1.enter_context(
            nc.sbuf_tensor(f"vst2_{i}", [128, 512], BF16)) for i in range(2)]

        NBLK = SPAN // BS
        NPIECE = BS // 512
        NTILE = BS // 128
        xbf_free = [0, 0]
        wkv_free = [0, 0]
        kbank_free = {0: 0, 2: 0, 4: 0}
        vbank_free = {1: 0, 3: 0, 5: 0}
        krot_free = [0, 0]
        vst_free = [0, 0]
        vst2_free = [0, 0]
        kt_free = [0, 0]
        cs_marks = [0, 0]
        cs_used = [0, 0]
        cskv_waited = [False, False]
        vpi = 0
        xnT_ready = [0, 0]

        def emit_ln_tile(blk, tt):
            bi = tt % 2
            lnm = ln_tile(dXkv, blk * BS + tt * 128, xbf[bi], xbf_free[bi])
            m, pem = transpose_tiles(xbf[bi], xnT2[blk % 2], BS, tt * 128,
                                     lnm, DC)
            xbf_free[bi] = pem
            if tt == NTILE - 1:
                xnT_ready[blk % 2] = m

        def emit_proj(blk, which, kc, wdram):
            nonlocal vpi
            xnT_blk = xnT2[blk % 2]
            idx = which * KVC + kc
            wbi = idx % 2
            wld = load_w_slice(dmaWKV, wkv_buf[wbi], wdram, kc * 128,
                               128, DC, waits=[(PE, wkv_free[wbi])])
            em.wait("pe", dmaWKV.sem, wld)
            em.wait("pe", ACT, xnT_ready[blk % 2])
            base = 0 if which == 0 else 1
            bis = [base + 2 * ((kc * NPIECE + piece) % 3)
                   for piece in range(NPIECE)]
            banks = [pb[b] for b in bis]
            for b in bis:
                em.wait("pe", DVE,
                        (kbank_free if which == 0 else vbank_free)[b])
            for ic in range(DC):
                for piece in range(NPIECE):
                    em.op("pe", lambda e, ic=ic, wbi=wbi,
                          piece=piece, bank=banks[piece], xnT_blk=xnT_blk:
                          e.matmul(
                              bank[:],
                              wkv_buf[wbi][:, ic * 128:ic * 128 + 128],
                              xnT_blk[:, ic * BS + piece * 512:
                                      ic * BS + piece * 512 + 512],
                              start=(ic == 0), stop=(ic == DC - 1)),
                          inc=PE)
            wkv_free[wbi] = PE.n
            for piece in range(NPIECE):
                bank_i = bis[piece]
                bank = pb[bank_i]
                projm = PE.n
                c0 = blk * BS + piece * 512
                cloc = piece * 512
                cbi = blk % 2
                if which == 0:
                    par = (kc * NPIECE + piece) % 2
                    kb = krot[par]
                    t1, t2, t3, t4 = kts[par]
                    em.wait("dve", PE, projm)
                    if not cskv_waited[cbi]:
                        em.wait("dve", dmaC.sem, cs_marks[cbi])
                        cskv_waited[cbi] = True
                    em.wait("dve", dmaKST.sem, krot_free[par])
                    em.wait("dve", DVE, kt_free[par])
                    for dst, rlo, clo in ((t1, 0, 0), (t2, 64, 64),
                                          (t3, 0, 64), (t4, 64, 0)):
                        em.op("dve", lambda e, dst=dst, rlo=rlo,
                              clo=clo, bank=bank, cloc=cloc, cbi=cbi:
                              e.tensor_tensor(
                                  dst[:], bank[rlo:rlo + 64, :],
                                  csb[cbi][clo:clo + 64, cloc:cloc + 512],
                                  ALU.mult), inc=DVE)
                    em.wait("dve", DVE, DVE.n)
                    em.op("dve", lambda e, kb=kb, t1=t1, t2=t2:
                          e.tensor_tensor(kb[0:64, :], t1[:], t2[:],
                                          ALU.subtract), inc=DVE)
                    em.op("dve", lambda e, kb=kb, t3=t3, t4=t4:
                          e.tensor_tensor(kb[64:128, :], t3[:], t4[:],
                                          ALU.add), inc=DVE)
                    kt_free[par] = DVE.n
                    kbank_free[bank_i] = DVE.n
                    cs_used[cbi] = DVE.n
                    krot_free[par] = dmaKST.start(
                        lambda e, kb=kb, kc=kc, c0=c0: e.dma_start(
                            out=dContrib[kc * 128:kc * 128 + 128,
                                         c0:c0 + 512],
                            in_=kb[:]), waits=[(DVE, DVE.n)])
                else:
                    par = vpi % 2
                    vb = vstage[par]
                    vb2 = vstage2[par]
                    em.wait("dve", PE, projm)
                    em.wait("dve", PE, vst_free[par])
                    em.op("dve", lambda e, vb=vb, bank=bank:
                          e.tensor_copy(vb[:], bank[:]), inc=DVE)
                    vbank_free[bank_i] = DVE.n
                    vevac = DVE.n
                    tbank_i = 6 + vpi % 2
                    tbank = pb[tbank_i]
                    em.wait("pe", DVE, vevac)
                    em.wait("pe", ACT, tb_free[tbank_i])
                    for j in range(4):
                        em.op("pe", lambda e, j=j, vb=vb, tbank=tbank:
                              e.matmul(
                                  tbank[:, 128 * j:128 * j + 128],
                                  vb[:, 128 * j:128 * j + 128],
                                  ident[:], start=True, stop=True),
                              inc=PE)
                    vst_free[par] = PE.n
                    em.wait("act", PE, PE.n)
                    em.wait("act", dmaVST.sem, vst2_free[par])
                    em.op("act", lambda e, vb2=vb2, tbank=tbank:
                          e.copy(vb2[:], tbank[:]), inc=ACT)
                    tb_free[tbank_i] = ACT.n
                    jj0 = c0 // 128
                    vst2_free[par] = dmaVST.start(
                        lambda e, vb2=vb2, kc=kc, jj0=jj0:
                        e.dma_start(
                            out=dContrib[DKV:DKV + SPAN, :].rearrange(
                                "(jj p) c -> p jj c", p=128)[
                                :, jj0:jj0 + 4, kc * 128:kc * 128 + 128],
                            in_=vb2[:].rearrange("p (j d) -> p j d", d=128)),
                        waits=[(ACT, ACT.n)])
                    vpi += 1

        projs = [(0, kc, dWk) for kc in range(KVC)] + \
                [(1, kc, dWv) for kc in range(KVC)]
        PPT = max(1, (2 * KVC) // NTILE)
        for blk in range(NBLK + 1):
            if blk < NBLK:
                em.wait("sp", DVE, cs_used[blk % 2])
                cs_marks[blk % 2] = dmaC.start(
                    lambda e, blk=blk: e.dma_start(
                        out=csb[blk % 2][:],
                        in_=dCkv[:, blk * BS:(blk + 1) * BS]))
                cskv_waited[blk % 2] = False
            pi = 0
            for tt in range(NTILE):
                if blk >= 1:
                    for _ in range(PPT):
                        if pi < len(projs):
                            w_, kc_, wd_ = projs[pi]
                            emit_proj(blk - 1, w_, kc_, wd_)
                            pi += 1
                if blk < NBLK:
                    emit_ln_tile(blk, tt)
            if blk >= 1:
                while pi < len(projs):
                    w_, kc_, wd_ = projs[pi]
                    emit_proj(blk - 1, w_, kc_, wd_)
                    pi += 1

        em.wait("pool", dmaKST.sem, dmaKST.sem.n)
        em.wait("pool", dmaVST.sem, dmaVST.sem.n)
        em.op("pool", lambda e: e.collective_compute(
            "AllGather", ALU.bypass,
            replica_groups=[list(range(NCORES))],
            ins=[dContrib.opt()], outs=[dGath.opt()]), inc=CC)
        barrier()

    with contextlib.ExitStack() as es2:
        xnT_mine = es2.enter_context(nc.sbuf_tensor("xnTm", [128, DC * TC],
                                                    BF16))
        with contextlib.ExitStack() as es2a:
            xbf2 = [es2a.enter_context(nc.sbuf_tensor(f"x2bf{i}", [128, D],
                                                      BF16)) for i in range(2)]
            xnTm_ready = build_xnT(dXq, TC, xnT_mine, xbf2, [0, 0])
            barrier()

        h_half = es2.enter_context(nc.sbuf_tensor("h_half", [128, FFH * TC],
                                                  BF16))
        ffn_done = 0
        for half in range(2):
            with contextlib.ExitStack() as es2b:
                wg_buf = [es2b.enter_context(
                    nc.sbuf_tensor(f"wg{half}_{i}", [128, DC * 128], BF16))
                    for i in range(2)]
                wu_buf = [es2b.enter_context(
                    nc.sbuf_tensor(f"wu{half}_{i}", [128, DC * 128], BF16))
                    for i in range(2)]
                silu_t = [es2b.enter_context(
                    nc.sbuf_tensor(f"silu{half}_{i}", [128, TP], F32))
                    for i in range(2)]
                wg_free = [PE.n, PE.n]
                wu_free = [PE.n, PE.n]
                silu_free = [0, 0]
                gu_bank_free = [DVE.n] * 4
                for oo in range(FFH):
                    o = half * FFH + oo
                    wbi = oo % 2
                    gld = load_w_slice(dmaWG, wg_buf[wbi], dWgT, o * 128, 128,
                                       DC, waits=[(PE, wg_free[wbi])])
                    uld = load_w_slice(dmaWU, wu_buf[wbi], dWuT, o * 128, 128,
                                       DC, waits=[(PE, wu_free[wbi])])
                    em.wait("pe", dmaWG.sem, gld)
                    em.wait("pe", dmaWU.sem, uld)
                    if oo == 0 and half == 0:
                        em.wait("pe", ACT, xnTm_ready)
                    for b in range(4):
                        em.wait("pe", DVE, gu_bank_free[b])
                    for ic in range(DC):
                        for tp in range(NTP):
                            em.op("pe", lambda e, ic=ic, tp=tp, wbi=wbi,
                                  wg_buf=wg_buf: e.matmul(
                                      pb[tp % 2][:, 0:TP],
                                      wg_buf[wbi][:, ic * 128:ic * 128 + 128],
                                      xnT_mine[:, ic * TC + tp * TP:
                                               ic * TC + tp * TP + TP],
                                      start=(ic == 0), stop=(ic == DC - 1)),
                                  inc=PE)
                        for tp in range(NTP):
                            em.op("pe", lambda e, ic=ic, tp=tp, wbi=wbi,
                                  wu_buf=wu_buf: e.matmul(
                                      pb[2 + tp % 2][:, 0:TP],
                                      wu_buf[wbi][:, ic * 128:ic * 128 + 128],
                                      xnT_mine[:, ic * TC + tp * TP:
                                               ic * TC + tp * TP + TP],
                                      start=(ic == 0), stop=(ic == DC - 1)),
                                  inc=PE)
                    gu_m = PE.n
                    wg_free[wbi] = gu_m
                    wu_free[wbi] = gu_m
                    for tp in range(NTP):
                        stl = silu_t[tp % 2]
                        em.wait("act", PE, gu_m)
                        em.wait("act", DVE, silu_free[tp % 2])
                        em.op("act", lambda e, stl=stl, tp=tp: e.activation(
                            stl[:, 0:TP], pb[tp % 2][:, 0:TP], AF.Silu,
                            bias=0.0, scale=1.0), inc=ACT)
                        em.wait("dve", ACT, ACT.n)
                        em.op("dve", lambda e, stl=stl, tp=tp, oo=oo:
                              e.tensor_tensor(
                                  h_half[:, oo * TC + tp * TP:
                                         oo * TC + tp * TP + TP],
                                  stl[:, 0:TP], pb[2 + tp % 2][:, 0:TP],
                                  ALU.mult), inc=DVE)
                        silu_free[tp % 2] = DVE.n
                        gu_bank_free[tp % 2] = DVE.n
                        gu_bank_free[2 + tp % 2] = DVE.n
                    em.wait("pe", ACT, ACT.n)
                h_ready = DVE.n
                barrier()

            with contextlib.ExitStack() as es2c:
                wd_buf = [es2c.enter_context(
                    nc.sbuf_tensor(f"wd{half}_{i}", [128, FFH * 128], BF16))
                    for i in range(2)]
                ystage = [es2c.enter_context(
                    nc.sbuf_tensor(f"yst{half}_{i}", [128, TC], F32))
                    for i in range(2)]
                fpart_t = [es2c.enter_context(
                    nc.sbuf_tensor(f"fpt{half}_{i}", [128, TC], F32))
                    for i in range(2)]
                wd_free = [PE.n, PE.n]
                ystage_free = [0, 0]
                fp_free = [0, 0]
                dn_bank_free = {4: DVE.n, 5: DVE.n}
                for od in range(DC):
                    wbi = od % 2
                    wld = dmaWD.start(
                        lambda e, wbi=wbi, od=od, half=half, wd_buf=wd_buf:
                        e.dma_start(
                            out=wd_buf[wbi][:].rearrange(
                                "p (fc c) -> p fc c", c=128),
                            in_=dWdT[half * FFH * 128:(half + 1) * FFH * 128,
                                     od * 128:od * 128 + 128].rearrange(
                                         "(fc p) c -> p fc c", p=128)),
                        waits=[(PE, wd_free[wbi])])
                    em.wait("pe", dmaWD.sem, wld)
                    if od == 0:
                        em.wait("pe", DVE, h_ready)
                    for tp in range(NTP):
                        em.wait("pe", DVE, dn_bank_free[4 + tp % 2])
                    for fc in range(FFH):
                        for tp in range(NTP):
                            em.op("pe", lambda e, fc=fc, tp=tp, wbi=wbi,
                                  wd_buf=wd_buf: e.matmul(
                                      pb[4 + tp % 2][:, 0:TP],
                                      wd_buf[wbi][:, fc * 128:fc * 128 + 128],
                                      h_half[:, fc * TC + tp * TP:
                                             fc * TC + tp * TP + TP],
                                      start=(fc == 0), stop=(fc == FFH - 1)),
                                  inc=PE)
                    dn_m = PE.n
                    wd_free[wbi] = dn_m
                    ys = ystage[od % 2]
                    em.wait("dve", PE, dn_m)
                    em.wait("dve", dmaYST.sem, ystage_free[od % 2])
                    if half == 0:
                        for tp in range(NTP):
                            em.op("dve", lambda e, ys=ys, tp=tp: e.tensor_copy(
                                ys[:, tp * TP:tp * TP + TP],
                                pb[4 + tp % 2][:, 0:TP]), inc=DVE)
                        for tp in range(NTP):
                            dn_bank_free[4 + tp % 2] = DVE.n
                        ystage_free[od % 2] = dmaYST.start(
                            lambda e, ys=ys, od=od: e.dma_start(
                                out=dFpart[od * 128:od * 128 + 128, :],
                                in_=ys[:]), waits=[(DVE, DVE.n)])
                    else:
                        fp = fpart_t[od % 2]
                        fld = dmaFPL.start(lambda e, fp=fp, od=od: e.dma_start(
                            out=fp[:], in_=dFpart[od * 128:od * 128 + 128, :]),
                            waits=[(DVE, fp_free[od % 2])])
                        em.wait("dve", dmaFPL.sem, fld)
                        for tp in range(NTP):
                            em.op("dve", lambda e, ys=ys, fp=fp, tp=tp:
                                  e.tensor_tensor(
                                      ys[:, tp * TP:tp * TP + TP],
                                      fp[:, tp * TP:tp * TP + TP],
                                      pb[4 + tp % 2][:, 0:TP], ALU.add),
                                  inc=DVE)
                        for tp in range(NTP):
                            dn_bank_free[4 + tp % 2] = DVE.n
                        fp_free[od % 2] = DVE.n
                        ystage_free[od % 2] = dmaYST.start(
                            lambda e, ys=ys, od=od: e.dma_start(
                                out=dFfn[od * 128:od * 128 + 128, :],
                                in_=ys[:]), waits=[(DVE, DVE.n)])
                if half == 1:
                    ffn_done = dmaYST.sem.n
                barrier()

    es3 = contextlib.ExitStack()
    qrotT = es3.enter_context(nc.sbuf_tensor("qrotT", [128, NH * TC], BF16))
    with contextlib.ExitStack() as es3a:
        xnT3 = es3a.enter_context(nc.sbuf_tensor("xnT3", [128, DC * TC],
                                                 BF16))
        xbf3 = [es3a.enter_context(nc.sbuf_tensor(f"x3bf{i}", [128, D], BF16))
                for i in range(2)]
        xnT3_ready = build_xnT(dXq, TC, xnT3, xbf3, [0, 0])
        csq = es3a.enter_context(nc.sbuf_tensor("csq", [128, TC], F32))
        csq_ld = dmaC.start(lambda e: e.dma_start(out=csq[:], in_=dCq))
        wq_buf = [es3a.enter_context(
            nc.sbuf_tensor(f"wq{i}", [128, DC * 128], BF16)) for i in range(2)]
        qts = [[es3a.enter_context(
            nc.sbuf_tensor(f"qt{j}_{i}", [64, TP], F32)) for j in range(4)]
            for i in range(2)]
        qt_free = [0, 0]
        wq_free = [PE.n, PE.n]
        qbank_free = {0: DVE.n, 1: DVE.n, 2: DVE.n, 3: DVE.n}
        csq_waited = {"dve": False}
        for hh in range(NH):
            wbi = hh % 2
            wld = load_w_slice(dmaWG, wq_buf[wbi], dWq, hh * 128, 128, DC,
                               waits=[(PE, wq_free[wbi])])
            em.wait("pe", dmaWG.sem, wld)
            if hh == 0:
                em.wait("pe", ACT, xnT3_ready)
            qb0 = 2 * (hh % 2)
            for tp in range(NTP):
                em.wait("pe", DVE, qbank_free[qb0 + tp % 2])
            for ic in range(DC):
                for tp in range(NTP):
                    em.op("pe", lambda e, ic=ic, tp=tp, wbi=wbi, qb0=qb0:
                          e.matmul(
                              pb[qb0 + tp % 2][:, 0:TP],
                              wq_buf[wbi][:, ic * 128:ic * 128 + 128],
                              xnT3[:, ic * TC + tp * TP:
                                   ic * TC + tp * TP + TP],
                              start=(ic == 0), stop=(ic == DC - 1)), inc=PE)
            wq_free[wbi] = PE.n
            for tp in range(NTP):
                bank_i = qb0 + tp % 2
                bank = pb[bank_i]
                t1, t2, t3, t4 = qts[tp % 2]
                em.wait("dve", PE, PE.n)
                if not csq_waited["dve"]:
                    em.wait("dve", dmaC.sem, csq_ld)
                    csq_waited["dve"] = True
                em.wait("dve", DVE, qt_free[tp % 2])
                c0 = tp * TP
                qcol = hh * TC + tp * TP
                for dst, rlo, clo in ((t1, 0, 0), (t2, 64, 64),
                                      (t3, 0, 64), (t4, 64, 0)):
                    em.op("dve", lambda e, dst=dst, rlo=rlo, clo=clo,
                          bank=bank, c0=c0: e.tensor_tensor(
                              dst[:, 0:TP], bank[rlo:rlo + 64, 0:TP],
                              csq[clo:clo + 64, c0:c0 + TP], ALU.mult),
                          inc=DVE)
                em.wait("dve", DVE, DVE.n)
                em.op("dve", lambda e, t1=t1, t2=t2, qcol=qcol:
                      e.tensor_tensor(qrotT[0:64, qcol:qcol + TP],
                                      t1[:, 0:TP], t2[:, 0:TP],
                                      ALU.subtract), inc=DVE)
                em.op("dve", lambda e, t3=t3, t4=t4, qcol=qcol:
                      e.tensor_tensor(qrotT[64:128, qcol:qcol + TP],
                                      t3[:, 0:TP], t4[:, 0:TP], ALU.add),
                      inc=DVE)
                qt_free[tp % 2] = DVE.n
                qbank_free[bank_i] = DVE.n
        qrot_ready = DVE.n
        barrier()

    with contextlib.ExitStack() as es4:
        ksb = es4.enter_context(nc.sbuf_tensor("ksb", [128, T], BF16))
        vsb = es4.enter_context(nc.sbuf_tensor("vsb", [128, T], BF16))
        sco2 = [es4.enter_context(nc.sbuf_tensor(f"sco{i}", [128, T], F32))
                for i in range(2)]
        psb2 = [es4.enter_context(nc.sbuf_tensor(f"psb{i}", [128, T], BF16))
                for i in range(2)]
        pt_sb = [es4.enter_context(nc.sbuf_tensor(f"pt{i}", [128, 512], BF16))
                 for i in range(2)]
        attst = [es4.enter_context(nc.sbuf_tensor(f"atts{i}", [128, G * 128],
                                                  BF16)) for i in range(2)]
        mx2 = [es4.enter_context(nc.sbuf_tensor(f"mx{i}", [128, 16], F32))
               for i in range(2)]
        negm2 = [es4.enter_context(nc.sbuf_tensor(f"ngm{i}", [128, 1], F32))
                 for i in range(2)]
        ssum2 = [es4.enter_context(nc.sbuf_tensor(f"ssm{i}", [128, 1], F32))
                 for i in range(2)]
        rs2 = [es4.enter_context(nc.sbuf_tensor(f"rs{i}", [128, 1], F32))
               for i in range(2)]
        ssp2 = [es4.enter_context(nc.sbuf_tensor(f"ssp{i}", [128, 4], F32))
                for i in range(2)]

        sc_free = {i: DVE.n for i in range(4)}
        ptb_free = {4: (DVE, DVE.n), 5: (DVE, DVE.n)}
        av_free = {6: DVE.n, 7: DVE.n}
        pt_free = [0, 0]
        psb_free = [0, 0]
        sco_free = [0, 0]
        att_free_t = [0, 0]
        exp_piece_marks = {}
        rs_marks = {}
        negm_marks = {}
        stage_info = {}

        def qk_emitters(i, t, g, h):
            pq = i % 2
            sco = sco2[pq]
            nk512 = 2 * (t + 1)
            q0 = (h * G + g) * TC + t * 128

            def one(ks):
                bank_i = ks % 4
                bank = pb[bank_i]
                em.wait("pe", DVE, sc_free[bank_i])
                diag = (ks >= nk512 - 2)
                em.op("pe", lambda e, ks=ks, bank=bank, q0=q0, diag=diag:
                      e.matmul(bank[:], qrotT[:, q0:q0 + 128],
                               ksb[:, ks * 512:ks * 512 + 512],
                               start=True, stop=not diag), inc=PE)
                if diag:
                    mof = (ks - (nk512 - 2)) * 512
                    em.op("pe", lambda e, bank=bank, mof=mof:
                          e.matmul(bank[:], ident[:], maskm[:, mof:mof + 512],
                                   start=False, stop=True), inc=PE)
                em.wait("dve", PE, PE.n)
                em.op("dve", lambda e, ks=ks, bank=bank, sco=sco, pq=pq:
                      e.tensor_scalar(
                          sco[:, ks * 512:ks * 512 + 512], bank[:],
                          0.0, None, ALU.add, ALU.max,
                          accum_out=mx2[pq][:, ks:ks + 1]), inc=DVE)
                sc_free[bank_i] = DVE.n
            return [lambda ks=ks: one(ks) for ks in range(nk512)]

        def qk_finish(i, t):
            pq = i % 2
            nk512 = 2 * (t + 1)
            em.wait("dve", DVE, DVE.n)
            em.op("dve", lambda e, nk512=nk512, pq=pq: e.tensor_reduce(
                negm2[pq][:], mx2[pq][:, 0:nk512], AX.X, ALU.max,
                negate=True), inc=DVE)
            negm_marks[i] = DVE.n

        def emit_sc2(i):
            t, g, h = stage_info[i]
            pq = i % 2
            np_ = 256 * (t + 1)
            em.wait("act", DVE, negm_marks[i])
            em.wait("act", PE, psb_free[pq])
            marks = []
            for p in range(4):
                em.op("act", lambda e, pq=pq, p=p, np_=np_: e.activation(
                    psb2[pq][:, p * np_:(p + 1) * np_],
                    sco2[pq][:, p * np_:(p + 1) * np_],
                    AF.Exp, bias=negm2[pq][:], scale=1.0,
                    accum_out=ssp2[pq][:, p:p + 1]), inc=ACT)
                marks.append(ACT.n)
            exp_piece_marks[i] = marks
            sco_free[pq] = marks[-1]

        class Trav:

            def __init__(self, i):
                self.i = i
                self.t, self.g, self.h = stage_info[i]
                self.pq = i % 2
                self.ncc = 8 * (self.t + 1)
                self.nb = self.ncc // 4
                self.avb_i = 6 + i % 2
                self.avbank = pb[self.avb_i]
                self.ptev_marks = [0, 0]
                self.av_waited = False

            def do_av(self, c4):
                pt = pt_sb[c4 % 2]
                esem = DVE if c4 % 2 == 0 else ACT
                em.wait("pe", esem, self.ptev_marks[c4 % 2])
                if not self.av_waited:
                    em.wait("pe", DVE, av_free[self.avb_i])
                    self.av_waited = True
                for j in range(4):
                    cc = c4 * 4 + j
                    em.op("pe", lambda e, cc=cc, pt=pt, s=self: e.matmul(
                        s.avbank[:, 0:128],
                        pt[:, 128 * (cc % 4):128 * (cc % 4) + 128],
                        vsb[:, 128 * cc:128 * cc + 128],
                        start=(cc == 0), stop=(cc == s.ncc - 1)), inc=PE)
                pt_free[c4 % 2] = PE.n

            def batch(self, c4):
                bank_i = 4 + c4 % 2
                bank = pb[bank_i]
                feng, fval = ptb_free[bank_i]
                em.wait("pe", feng, fval)
                pneed = -(-2 * (c4 + 1) // (self.t + 1)) - 1
                em.wait("pe", ACT, exp_piece_marks[self.i][pneed])
                for j in range(4):
                    cc = c4 * 4 + j
                    em.op("pe", lambda e, cc=cc, bank=bank, j=j, s=self:
                          e.matmul(
                              bank[:, 128 * j:128 * j + 128],
                              psb2[s.pq][:, 128 * cc:128 * cc + 128],
                              ident[:], start=True, stop=True), inc=PE)
                tr_m = PE.n
                if c4 > 0:
                    self.do_av(c4 - 1)
                pt = pt_sb[c4 % 2]
                eng = "dve" if c4 % 2 == 0 else "act"
                esem = DVE if c4 % 2 == 0 else ACT
                em.wait(eng, PE, tr_m)
                em.wait(eng, PE, pt_free[c4 % 2])
                if eng == "dve":
                    em.op("dve", lambda e, pt=pt, bank=bank: e.tensor_copy(
                        pt[:], bank[:]), inc=DVE)
                else:
                    em.op("act", lambda e, pt=pt, bank=bank: e.copy(
                        pt[:], bank[:]), inc=ACT)
                ptb_free[bank_i] = (esem, esem.n)
                self.ptev_marks[c4 % 2] = esem.n

            def finish(self):
                self.do_av(self.nb - 1)
                psb_free[self.pq] = PE.n
                pq = self.pq
                em.wait("dve", ACT, exp_piece_marks[self.i][-1])
                em.op("dve", lambda e, pq=pq: e.reduce_sum(
                    ssum2[pq][:], ssp2[pq][:], AX.X), inc=DVE)
                em.wait("dve", DVE, DVE.n)
                em.op("dve", lambda e, pq=pq: e.reciprocal(
                    rs2[pq][:], ssum2[pq][:]), inc=DVE)
                em.wait("dve", DVE, DVE.n)
                em.wait("dve", PE, PE.n)
                em.wait("dve", dmaATT.sem, att_free_t[self.t % 2])
                t, g, h = self.t, self.g, self.h
                em.op("dve", lambda e, g=g, t=t, s=self, pq=self.pq:
                      e.tensor_scalar(
                          attst[t % 2][:, g * 128:g * 128 + 128],
                          s.avbank[:, 0:128], rs2[pq][:], None, ALU.mult),
                      inc=DVE)
                av_free[self.avb_i] = DVE.n
                if g == G - 1:
                    att_free_t[t % 2] = dmaATT.start(
                        lambda e, h=h, t=t: e.dma_start(
                            out=dAtt[t * 128:t * 128 + 128,
                                     h * G * 128:(h + 1) * G * 128],
                            in_=attst[t % 2][:]),
                        waits=[(DVE, DVE.n)])

        def emit_pair(i, t, g, h, prev):
            pq = i % 2
            em.wait("dve", ACT, sco_free[pq])
            qks = qk_emitters(i, t, g, h)
            tr = Trav(prev) if prev is not None else None
            lead = t + 1
            for k in range(lead):
                qks[k]()
            bi_ = 0
            for k in range(lead, len(qks)):
                qks[k]()
                for _ in range(2):
                    if tr is not None and bi_ < tr.nb:
                        tr.batch(bi_)
                        bi_ += 1
            while tr is not None and bi_ < tr.nb:
                tr.batch(bi_)
                bi_ += 1
            qk_finish(i, t)
            if tr is not None:
                tr.finish()
            stage_info[i] = (t, g, h)

        prev = None
        for h in range(NKV):
            if prev is not None:
                tr = Trav(prev)
                for c4 in range(tr.nb):
                    tr.batch(c4)
                tr.finish()
                prev = None
            QR = DKV + SPAN
            kld = dmaKVL.start(lambda e, h=h: e.dma_start(
                out=ksb[:].rearrange("p (r c) -> p r c", c=SPAN),
                in_=dGath.rearrange("(r q) c -> q r c", q=QR)[
                    h * 128:h * 128 + 128]),
                waits=([(CC, 1), (PE, PE.n)] if h == 0
                       else [(PE, PE.n)]))
            for r in range(NCORES):
                vld = dmaKVL.start(lambda e, h=h, r=r: e.dma_start(
                    out=vsb[:, r * SPAN:(r + 1) * SPAN].rearrange(
                        "p (jj d) -> p jj d", d=128),
                    in_=dGath[r * QR + DKV:(r + 1) * QR,
                              h * 128:h * 128 + 128].rearrange(
                        "(jj p) c -> p jj c", p=128)))
            em.wait("pe", dmaKVL.sem, vld)
            if h == 0:
                em.wait("pe", DVE, qrot_ready)
            for t in range(NQB):
                for g in range(G):
                    i = (h * NQB + t) * G + g
                    stage_info[i] = (t, g, h)
                    emit_pair(i, t, g, h, prev)
                    emit_sc2(i)
                    prev = i
        tr = Trav(prev)
        for c4 in range(tr.nb):
            tr.batch(c4)
        tr.finish()
        barrier()
    es3.close()
    barrier()

    with contextlib.ExitStack() as es5:
        attT = es5.enter_context(nc.sbuf_tensor("attT", [128, DC * TC], BF16))
        wo_buf = [es5.enter_context(
            nc.sbuf_tensor(f"wo{i}", [128, DC * 128], BF16)) for i in range(2)]
        xr_t = [es5.enter_context(nc.sbuf_tensor(f"xr{i}", [128, TC], F32))
                for i in range(2)]
        fr_t = [es5.enter_context(nc.sbuf_tensor(f"fr{i}", [128, TC], F32))
                for i in range(2)]
        o1_t = [es5.enter_context(nc.sbuf_tensor(f"o1{i}", [128, TC], F32))
                for i in range(2)]

        att_tile = [es5.enter_context(nc.sbuf_tensor(f"at5_{i}", [128, D],
                                                     BF16)) for i in range(2)]
        at_free = [PE.n, PE.n]
        attT_ready = 0
        for tt in range(TC // 128):
            bi = tt % 2
            ld = dmaATT.start(lambda e, tt=tt, bi=bi: e.dma_start(
                out=att_tile[bi][:], in_=dAtt[tt * 128:tt * 128 + 128, :]),
                waits=[(PE, at_free[bi])])
            em.wait("pe", dmaATT.sem, ld)
            attT_ready, pem = transpose_tiles(att_tile[bi], attT, TC,
                                              tt * 128, 0, DC)
            at_free[bi] = pem
        wo_free = [PE.n, PE.n]
        obank_free = {0: DVE.n, 1: DVE.n}
        res_marks = [0, 0]
        out_marks = [0, 0]
        for od in range(DC):
            wbi = od % 2
            wld = load_w_slice(dmaWG, wo_buf[wbi], dWo, od * 128, 128, DC,
                               waits=[(PE, wo_free[wbi])])
            xr, fr, o1 = xr_t[od % 2], fr_t[od % 2], o1_t[od % 2]
            xld = dmaXR.start(lambda e, xr=xr, od=od: e.dma_start(
                out=xr[:], in_=dXqT[od * 128:od * 128 + 128, :]),
                waits=[(DVE, res_marks[od % 2])])
            fld = dmaFR.start(lambda e, fr=fr, od=od: e.dma_start(
                out=fr[:], in_=dFfn[od * 128:od * 128 + 128, :]),
                waits=([(dmaYST.sem, ffn_done), (DVE, res_marks[od % 2])]
                       if od == 0 else [(DVE, res_marks[od % 2])]))
            em.wait("pe", dmaWG.sem, wld)
            if od == 0:
                em.wait("pe", ACT, attT_ready)
            for tp in range(NTP):
                em.wait("pe", DVE, obank_free[tp % 2])
            for ic in range(DC):
                for tp in range(NTP):
                    em.op("pe", lambda e, ic=ic, tp=tp, wbi=wbi: e.matmul(
                        pb[tp % 2][:, 0:TP],
                        wo_buf[wbi][:, ic * 128:ic * 128 + 128],
                        attT[:, ic * TC + tp * TP: ic * TC + tp * TP + TP],
                        start=(ic == 0), stop=(ic == DC - 1)), inc=PE)
            wo_free[wbi] = PE.n
            em.wait("dve", PE, PE.n)
            em.wait("dve", dmaXR.sem, xld)
            em.wait("dve", dmaFR.sem, fld)
            em.wait("dve", dmaOUT.sem, out_marks[od % 2])
            for tp in range(NTP):
                em.op("dve", lambda e, o1=o1, xr=xr, tp=tp: e.tensor_tensor(
                    o1[:, tp * TP:tp * TP + TP], xr[:, tp * TP:tp * TP + TP],
                    pb[tp % 2][:, 0:TP], ALU.add), inc=DVE)
                obank_free[tp % 2] = DVE.n
            em.wait("dve", DVE, DVE.n)
            em.op("dve", lambda e, o1=o1, fr=fr: e.tensor_tensor(
                o1[:], o1[:], fr[:], ALU.add), inc=DVE)
            res_marks[od % 2] = DVE.n
            out_marks[od % 2] = dmaOUT.start(
                lambda e, o1=o1, od=od: e.dma_start(
                    out=dOut[od * 128:od * 128 + 128, :], in_=o1[:]),
                waits=[(DVE, DVE.n)])
        em.wait("sp", dmaOUT.sem, dmaOUT.sem.n)

    em.flush(nc)
    es.close()
    return nc



def _host_inputs(cfg, core, X, Wq, Wk, Wv, Wo, Wg, Wu, Wd):
    T = cfg.T
    f32 = np.float32
    bf16 = ml_dtypes.bfloat16
    Xq = np.ascontiguousarray(X[core::NCORES])
    XqT = np.ascontiguousarray(Xq.T)
    Xkv = np.ascontiguousarray(X[core * SPAN:(core + 1) * SPAN])

    half = DH // 2
    inv = 1.0 / (10000.0 ** (np.arange(half, dtype=np.float64) / half))
    pos_blk = np.arange(core * SPAN, (core + 1) * SPAN, dtype=np.float64)
    ang = pos_blk[None, :] * inv[:, None]
    cs_kv = np.concatenate([np.cos(ang), np.sin(ang)], axis=0).astype(f32)
    pos_q = np.arange(T, dtype=np.float64)[core::NCORES]
    angq = pos_q[None, :] * inv[:, None]
    s = 1.0 / np.sqrt(np.float64(DH))
    cs_q = np.concatenate([np.cos(angq) * s, np.sin(angq) * s],
                          axis=0).astype(f32)

    mp = np.arange(SPAN)[None, :] <= (NCORES * np.arange(QBLK)[:, None] + core)
    maskm = np.where(mp, 0.0, -30000.0).astype(f32).astype(bf16)
    ident = np.eye(128, dtype=f32).astype(bf16)

    return {
        "Xkv": Xkv, "Xq": Xq, "XqT": XqT,
        "Wq": Wq, "Wk": Wk, "Wv": Wv, "Wo": Wo,
        "WgT": np.ascontiguousarray(Wg.T),
        "WuT": np.ascontiguousarray(Wu.T),
        "WdT": np.ascontiguousarray(Wd.T),
        "cs_kv": cs_kv, "cs_q": cs_q, "maskm": maskm, "ident": ident,
    }


def run(cfg, X, Wq, Wk, Wv, Wo, Wg, Wu, Wd, trace=False):
    nc = build_program(cfg)
    in_maps = [_host_inputs(cfg, c, X, Wq, Wk, Wv, Wo, Wg, Wu, Wd)
               for c in range(NCORES)]
    res = run_bass_kernel_spmd(nc, in_maps, core_ids=list(range(NCORES)),
                               trace=trace)
    out = np.empty((cfg.T, cfg.D), np.float32)
    for c in range(NCORES):
        out[c::NCORES, :] = res.results[c]["outT"].T
    return out, res


def kernel(L0_attn_X, L0_attn_Wq, L0_attn_Wk, L0_attn_Wv, L0_attn_Wo,
           L0_ffn_W_gate, L0_ffn_W_up, L0_ffn_W_down):
    cfg = Cfg()
    X = np.ascontiguousarray(np.asarray(L0_attn_X, np.float32).reshape(
        cfg.T, cfg.D))
    out, _ = run(cfg, X,
                 np.asarray(L0_attn_Wq, np.float32),
                 np.asarray(L0_attn_Wk, np.float32),
                 np.asarray(L0_attn_Wv, np.float32),
                 np.asarray(L0_attn_Wo, np.float32),
                 np.asarray(L0_ffn_W_gate, np.float32),
                 np.asarray(L0_ffn_W_up, np.float32),
                 np.asarray(L0_ffn_W_down, np.float32))
    return out.reshape(1, cfg.T, cfg.D)


# version 09
# speedup vs baseline: 1.7331x; 1.3024x over previous
import contextlib
import numpy as np
import ml_dtypes
import concourse.bass as bass
import concourse.mybir as mybir
from concourse.bass_utils import run_bass_kernel_spmd

F32 = mybir.dt.float32
BF16 = mybir.dt.bfloat16
AF = mybir.ActivationFunctionType
ALU = mybir.AluOpType
AX = mybir.AxisListType

NCORES = 8
DH = 128
QBLK = 128
SPAN = NCORES * QBLK


class Cfg:
    def __init__(self, T=8192, D=4096, DFF=11008, NH=32, NKV=8):
        self.T, self.D, self.DFF, self.NH, self.NKV = T, D, DFF, NH, NKV
        self.TC = T // NCORES
        self.NQB = self.TC // QBLK
        self.DC = D // 128
        self.DKV = NKV * DH
        self.KVC = self.DKV // 128
        self.FFC = DFF // 128
        self.FFH = self.FFC // 2
        self.G = NH // NKV
        self.NT = T // 128
        assert self.TC % QBLK == 0 and self.FFC % 2 == 0 and D % 128 == 0
        assert T % SPAN == 0 and NH % NKV == 0 and self.NT % 4 == 0


class Sem:
    def __init__(self, es, nc, name, step=1):
        self.h = es.enter_context(nc.semaphore(name))
        self.n = 0
        self.step = step


class Emitter:
    ENGINES = ("pe", "dve", "act", "pool", "sp")

    def __init__(self):
        self.ops = {e: [] for e in self.ENGINES}

    def op(self, eng, fn, inc=None, k=1):
        if inc is not None:
            def wrapped(e, fn=fn, s=inc, k=k):
                fn(e).then_inc(s.h, k * s.step)
            self.ops[eng].append(wrapped)
            inc.n += k * inc.step
            return inc.n
        self.ops[eng].append(fn)
        return None

    def wait(self, eng, sem, val):
        if val > 0:
            self.ops[eng].append(lambda e, h=sem.h, v=val: e.wait_ge(h, v))

    def flush(self, nc):
        with nc.Block() as block:
            @block.tensor
            def _(t):
                for f in self.ops["pe"]:
                    f(t)

            @block.vector
            def _(v):
                for f in self.ops["dve"]:
                    f(v)

            @block.scalar
            def _(s):
                for f in self.ops["act"]:
                    f(s)

            @block.gpsimd
            def _(g):
                for f in self.ops["pool"]:
                    f(g)

            @block.sync
            def _(sy):
                for f in self.ops["sp"]:
                    f(sy)


class Dma:

    def __init__(self, es, nc, em, name, queue):
        self.sem = Sem(es, nc, name, step=16)
        self.em = em
        self.q = queue

    def start(self, fn, waits=()):
        em = self.em
        em.wait(self.q, self.sem, self.sem.n)
        for (s, v) in waits:
            em.wait(self.q, s, v)
        em.op(self.q, fn, inc=self.sem)
        return self.sem.n


def build_program(cfg):
    nc = bass.Bass("TRN2", target_bir_lowering=False, debug=False)
    T, D, DFF, NH, NKV = cfg.T, cfg.D, cfg.DFF, cfg.NH, cfg.NKV
    TC, DC, DKV, KVC, FFH, G = (cfg.TC, cfg.DC, cfg.DKV, cfg.KVC,
                                cfg.FFH, cfg.G)
    NQB, NT = cfg.NQB, cfg.NT
    TP = min(TC, 512)
    NTP = TC // TP

    dXkv = nc.dram_tensor("Xkv", [SPAN, D], F32, kind="ExternalInput").ap()
    dXq = nc.dram_tensor("Xq", [TC, D], F32, kind="ExternalInput").ap()
    dXqT = nc.dram_tensor("XqT", [D, TC], F32, kind="ExternalInput").ap()
    dWq = nc.dram_tensor("Wq", [D, D], F32, kind="ExternalInput").ap()
    dWk = nc.dram_tensor("Wk", [D, DKV], F32, kind="ExternalInput").ap()
    dWv = nc.dram_tensor("Wv", [D, DKV], F32, kind="ExternalInput").ap()
    dWo = nc.dram_tensor("Wo", [D, D], F32, kind="ExternalInput").ap()
    dWgT = nc.dram_tensor("WgT", [D, DFF], F32, kind="ExternalInput").ap()
    dWuT = nc.dram_tensor("WuT", [D, DFF], F32, kind="ExternalInput").ap()
    dWdT = nc.dram_tensor("WdT", [DFF, D], F32, kind="ExternalInput").ap()
    dCkv = nc.dram_tensor("cs_kv", [128, SPAN], F32, kind="ExternalInput").ap()
    dCq = nc.dram_tensor("cs_q", [128, TC], F32, kind="ExternalInput").ap()
    dMask = nc.dram_tensor("maskm", [128, SPAN], BF16, kind="ExternalInput").ap()
    dIdent = nc.dram_tensor("ident", [128, 128], BF16, kind="ExternalInput").ap()
    dOut = nc.dram_tensor("outT", [D, TC], F32, kind="ExternalOutput").ap()

    dContribT = nc.dram_tensor("kvcontrib", [DKV + SPAN, SPAN], BF16,
                               kind="Internal")
    dGathT = nc.dram_tensor("kvgath", [NCORES * (DKV + SPAN), SPAN], BF16,
                            kind="Internal")
    dContrib = dContribT.ap()
    dGath = dGathT.ap()
    dFpart = nc.dram_tensor("ffnpart", [D, TC], F32, kind="Internal").ap()
    dFfn = nc.dram_tensor("ffnscr", [D, TC], F32, kind="Internal").ap()
    dAtt = nc.dram_tensor("attscr", [TC, D], BF16, kind="Internal").ap()

    es = contextlib.ExitStack()
    em = Emitter()

    PE = Sem(es, nc, "s_pe")
    DVE = Sem(es, nc, "s_dve")
    ACT = Sem(es, nc, "s_act")
    POOL = Sem(es, nc, "s_pool")
    CC = Sem(es, nc, "s_cc", step=1)

    dmaC = Dma(es, nc, em, "d_const", "sp")
    dmaX = Dma(es, nc, em, "d_x", "pool")
    dmaWKV = Dma(es, nc, em, "d_wkv", "pool")
    dmaWG = Dma(es, nc, em, "d_wg", "pool")
    dmaWU = Dma(es, nc, em, "d_wu", "pool")
    dmaWD = Dma(es, nc, em, "d_wd", "pool")
    dmaKST = Dma(es, nc, em, "d_kst", "sp")
    dmaVST = Dma(es, nc, em, "d_vst", "sp")
    dmaYST = Dma(es, nc, em, "d_yst", "sp")
    dmaFPL = Dma(es, nc, em, "d_fpl", "sp")
    dmaKVL = Dma(es, nc, em, "d_kvl", "pool")
    dmaATT = Dma(es, nc, em, "d_att", "sp")
    dmaXR = Dma(es, nc, em, "d_xr", "sp")
    dmaFR = Dma(es, nc, em, "d_fr", "sp")
    dmaOUT = Dma(es, nc, em, "d_out", "sp")
    all_sems = [PE, DVE, ACT, POOL] + [d.sem for d in (
        dmaC, dmaX, dmaWKV, dmaWG, dmaWU, dmaWD, dmaKST, dmaVST, dmaYST,
        dmaFPL, dmaKVL, dmaATT, dmaXR, dmaFR, dmaOUT)]

    def barrier():
        for eng in em.ENGINES:
            for s in all_sems:
                em.wait(eng, s, s.n)

    pb = [es.enter_context(nc.psum_tensor(f"pb{i}", [128, 512], F32))
          for i in range(8)]

    ident = es.enter_context(nc.sbuf_tensor("ident_sb", [128, 128], BF16))
    maskm = es.enter_context(nc.sbuf_tensor("mask_sb", [128, SPAN], BF16))
    eps_t = es.enter_context(nc.sbuf_tensor("eps_sb", [128, 1], F32))
    st = {
        "bn": es.enter_context(nc.sbuf_tensor("st_bn", [128, 6 * (D // 512)],
                                              F32)),
        "ag": es.enter_context(nc.sbuf_tensor("st_ag", [128, 2], F32)),
        "sq": es.enter_context(nc.sbuf_tensor("st_sq", [128, 1], F32)),
        "rs": es.enter_context(nc.sbuf_tensor("st_rs", [128, 1], F32)),
    }

    dmaC.start(lambda e: e.dma_start(out=ident[:], in_=dIdent))
    const_ld = dmaC.start(lambda e: e.dma_start(out=maskm[:], in_=dMask))
    em.op("pool", lambda e: e.memset(eps_t[:], 1e-5), inc=POOL)
    eps_ready = POOL.n
    em.wait("pe", dmaC.sem, const_ld)
    em.wait("dve", dmaC.sem, const_ld)

    ln_state = {"first": True}

    def ln_tile(xsrc, row0, buf, free_mark):
        dmaX.start(
            lambda e: e.dma_start(out=buf[:], in_=xsrc[row0:row0 + 128, :]),
            waits=[(PE, free_mark)])
        em.wait("dve", dmaX.sem, dmaX.sem.n)
        nblk = D // 512
        for b in range(nblk):
            em.op("dve", lambda e, b=b, buf=buf: e.bn_stats(
                st["bn"][:, 6 * b:6 * b + 6], buf[:, 512 * b:512 * b + 512]),
                inc=DVE)
        em.wait("dve", DVE, DVE.n)
        em.op("dve", lambda e: e.bn_aggr(st["ag"][:], st["bn"][:, 0:6 * nblk]),
              inc=DVE)
        em.wait("act", DVE, DVE.n)
        if ln_state["first"]:
            em.wait("act", POOL, eps_ready)
            ln_state["first"] = False
        em.op("act", lambda e: e.activation(st["sq"][:], st["ag"][:, 1:2],
                                            AF.Sqrt, bias=eps_t[:], scale=1.0),
              inc=ACT)
        em.wait("dve", ACT, ACT.n)
        em.op("dve", lambda e: e.reciprocal(st["rs"][:], st["sq"][:]), inc=DVE)
        em.wait("dve", DVE, DVE.n)
        em.op("dve", lambda e, buf=buf: e.tensor_scalar(
            buf[:], buf[:], st["ag"][:, 0:1], st["rs"][:],
            ALU.subtract, ALU.mult), inc=DVE)
        return DVE.n

    tb_free = {6: 0, 7: 0}

    def transpose_tiles(src, dst, ncols, col0, ln_mark, nch_total):
        em.wait("pe", DVE, ln_mark)
        dview = dst[:].rearrange("p (ch c) -> p ch c", c=ncols)
        for q4 in range((nch_total + 3) // 4):
            bank_i = 6 + q4 % 2
            bank = pb[bank_i]
            nch = min(4, nch_total - 4 * q4)
            em.wait("pe", ACT, tb_free[bank_i])
            for j in range(nch):
                ch = 4 * q4 + j
                em.op("pe", lambda e, ch=ch, bank=bank, j=j, src=src: e.matmul(
                    bank[:, 128 * j:128 * j + 128],
                    src[:, 128 * ch:128 * ch + 128], ident[:],
                    start=True, stop=True), inc=PE)
            em.wait("act", PE, PE.n)
            em.op("act", lambda e, bank=bank, q4=q4, nch=nch, dview=dview:
                  e.copy(dview[:, 4 * q4:4 * q4 + nch, col0:col0 + 128],
                         bank[:].rearrange("p (j c) -> p j c", c=128)[:, 0:nch, :]),
                  inc=ACT)
            tb_free[bank_i] = ACT.n
        return ACT.n, PE.n

    def load_w_slice(dma, dst, wdram, col0, ncols, nch, waits=()):
        return dma.start(lambda e: e.dma_start(
            out=dst[:, 0:nch * ncols].rearrange("p (ic c) -> p ic c", c=ncols),
            in_=wdram[0:nch * 128, col0:col0 + ncols].rearrange(
                "(ic p) c -> p ic c", p=128)), waits=waits)

    def build_xnT(xsrc, ntok, xnT, xbf_bufs, xbf_free):
        m = 0
        for tt in range(ntok // 128):
            bi = tt % 2
            lnm = ln_tile(xsrc, tt * 128, xbf_bufs[bi], xbf_free[bi])
            m, pem = transpose_tiles(xbf_bufs[bi], xnT, ntok, tt * 128,
                                     lnm, DC)
            xbf_free[bi] = pem
        return m

    BS = SPAN
    with contextlib.ExitStack() as es1:
        xnT2 = [es1.enter_context(nc.sbuf_tensor(f"p1xnT{i}", [128, DC * BS],
                                                 BF16)) for i in range(2)]
        xbf = [es1.enter_context(nc.sbuf_tensor(f"xbf{i}", [128, D], BF16))
               for i in range(2)]
        wkv_buf = [es1.enter_context(
            nc.sbuf_tensor(f"wkv{i}", [128, DC * 128], BF16)) for i in range(2)]
        csb = [es1.enter_context(nc.sbuf_tensor(f"csb{i}", [128, BS], F32))
               for i in range(2)]
        krot = [es1.enter_context(nc.sbuf_tensor(f"krot{i}", [128, 512], BF16))
                for i in range(2)]
        kts = [[es1.enter_context(nc.sbuf_tensor(f"kt{j}_{i}", [64, 512], F32))
                for j in range(4)] for i in range(2)]
        vstage = [es1.enter_context(
            nc.sbuf_tensor(f"vst{i}", [128, 512], BF16)) for i in range(2)]
        vstage2 = [es1.enter_context(
            nc.sbuf_tensor(f"vst2_{i}", [128, 512], BF16)) for i in range(2)]

        NBLK = SPAN // BS
        NPIECE = BS // 512
        NTILE = BS // 128
        xbf_free = [0, 0]
        wkv_free = [0, 0]
        kbank_free = {0: 0, 2: 0, 4: 0}
        vbank_free = {1: 0, 3: 0, 5: 0}
        krot_free = [0, 0]
        vst_free = [0, 0]
        vst2_free = [0, 0]
        kt_free = [0, 0]
        cs_marks = [0, 0]
        cs_used = [0, 0]
        cskv_waited = [False, False]
        vpi = 0
        xnT_ready = [0, 0]

        def emit_ln_tile(blk, tt):
            bi = tt % 2
            lnm = ln_tile(dXkv, blk * BS + tt * 128, xbf[bi], xbf_free[bi])
            m, pem = transpose_tiles(xbf[bi], xnT2[blk % 2], BS, tt * 128,
                                     lnm, DC)
            xbf_free[bi] = pem
            if tt == NTILE - 1:
                xnT_ready[blk % 2] = m

        def emit_proj(blk, which, kc, wdram):
            nonlocal vpi
            xnT_blk = xnT2[blk % 2]
            idx = which * KVC + kc
            wbi = idx % 2
            wld = load_w_slice(dmaWKV, wkv_buf[wbi], wdram, kc * 128,
                               128, DC, waits=[(PE, wkv_free[wbi])])
            em.wait("pe", dmaWKV.sem, wld)
            em.wait("pe", ACT, xnT_ready[blk % 2])
            base = 0 if which == 0 else 1
            bis = [base + 2 * ((kc * NPIECE + piece) % 3)
                   for piece in range(NPIECE)]
            banks = [pb[b] for b in bis]
            for b in bis:
                em.wait("pe", DVE,
                        (kbank_free if which == 0 else vbank_free)[b])
            for ic in range(DC):
                for piece in range(NPIECE):
                    em.op("pe", lambda e, ic=ic, wbi=wbi,
                          piece=piece, bank=banks[piece], xnT_blk=xnT_blk:
                          e.matmul(
                              bank[:],
                              wkv_buf[wbi][:, ic * 128:ic * 128 + 128],
                              xnT_blk[:, ic * BS + piece * 512:
                                      ic * BS + piece * 512 + 512],
                              start=(ic == 0), stop=(ic == DC - 1)),
                          inc=PE)
            wkv_free[wbi] = PE.n
            for piece in range(NPIECE):
                bank_i = bis[piece]
                bank = pb[bank_i]
                projm = PE.n
                c0 = blk * BS + piece * 512
                cloc = piece * 512
                cbi = blk % 2
                if which == 0:
                    par = (kc * NPIECE + piece) % 2
                    kb = krot[par]
                    t1, t2, t3, t4 = kts[par]
                    em.wait("dve", PE, projm)
                    if not cskv_waited[cbi]:
                        em.wait("dve", dmaC.sem, cs_marks[cbi])
                        cskv_waited[cbi] = True
                    em.wait("dve", dmaKST.sem, krot_free[par])
                    em.wait("dve", DVE, kt_free[par])
                    for dst, rlo, clo in ((t1, 0, 0), (t2, 64, 64),
                                          (t3, 0, 64), (t4, 64, 0)):
                        em.op("dve", lambda e, dst=dst, rlo=rlo,
                              clo=clo, bank=bank, cloc=cloc, cbi=cbi:
                              e.tensor_tensor(
                                  dst[:], bank[rlo:rlo + 64, :],
                                  csb[cbi][clo:clo + 64, cloc:cloc + 512],
                                  ALU.mult), inc=DVE)
                    em.wait("dve", DVE, DVE.n)
                    em.op("dve", lambda e, kb=kb, t1=t1, t2=t2:
                          e.tensor_tensor(kb[0:64, :], t1[:], t2[:],
                                          ALU.subtract), inc=DVE)
                    em.op("dve", lambda e, kb=kb, t3=t3, t4=t4:
                          e.tensor_tensor(kb[64:128, :], t3[:], t4[:],
                                          ALU.add), inc=DVE)
                    kt_free[par] = DVE.n
                    kbank_free[bank_i] = DVE.n
                    cs_used[cbi] = DVE.n
                    krot_free[par] = dmaKST.start(
                        lambda e, kb=kb, kc=kc, c0=c0: e.dma_start(
                            out=dContrib[kc * 128:kc * 128 + 128,
                                         c0:c0 + 512],
                            in_=kb[:]), waits=[(DVE, DVE.n)])
                else:
                    par = vpi % 2
                    vb = vstage[par]
                    vb2 = vstage2[par]
                    em.wait("dve", PE, projm)
                    em.wait("dve", PE, vst_free[par])
                    em.op("dve", lambda e, vb=vb, bank=bank:
                          e.tensor_copy(vb[:], bank[:]), inc=DVE)
                    vbank_free[bank_i] = DVE.n
                    vevac = DVE.n
                    tbank_i = 6 + vpi % 2
                    tbank = pb[tbank_i]
                    em.wait("pe", DVE, vevac)
                    em.wait("pe", ACT, tb_free[tbank_i])
                    for j in range(4):
                        em.op("pe", lambda e, j=j, vb=vb, tbank=tbank:
                              e.matmul(
                                  tbank[:, 128 * j:128 * j + 128],
                                  vb[:, 128 * j:128 * j + 128],
                                  ident[:], start=True, stop=True),
                              inc=PE)
                    vst_free[par] = PE.n
                    em.wait("act", PE, PE.n)
                    em.wait("act", dmaVST.sem, vst2_free[par])
                    em.op("act", lambda e, vb2=vb2, tbank=tbank:
                          e.copy(vb2[:], tbank[:]), inc=ACT)
                    tb_free[tbank_i] = ACT.n
                    jj0 = c0 // 128
                    vst2_free[par] = dmaVST.start(
                        lambda e, vb2=vb2, kc=kc, jj0=jj0:
                        e.dma_start(
                            out=dContrib[DKV:DKV + SPAN, :].rearrange(
                                "(jj p) c -> p jj c", p=128)[
                                :, jj0:jj0 + 4, kc * 128:kc * 128 + 128],
                            in_=vb2[:].rearrange("p (j d) -> p j d", d=128)),
                        waits=[(ACT, ACT.n)])
                    vpi += 1

        projs = [(0, kc, dWk) for kc in range(KVC)] + \
                [(1, kc, dWv) for kc in range(KVC)]
        PPT = max(1, (2 * KVC) // NTILE)
        for blk in range(NBLK + 1):
            if blk < NBLK:
                em.wait("sp", DVE, cs_used[blk % 2])
                cs_marks[blk % 2] = dmaC.start(
                    lambda e, blk=blk: e.dma_start(
                        out=csb[blk % 2][:],
                        in_=dCkv[:, blk * BS:(blk + 1) * BS]))
                cskv_waited[blk % 2] = False
            pi = 0
            for tt in range(NTILE):
                if blk >= 1:
                    for _ in range(PPT):
                        if pi < len(projs):
                            w_, kc_, wd_ = projs[pi]
                            emit_proj(blk - 1, w_, kc_, wd_)
                            pi += 1
                if blk < NBLK:
                    emit_ln_tile(blk, tt)
            if blk >= 1:
                while pi < len(projs):
                    w_, kc_, wd_ = projs[pi]
                    emit_proj(blk - 1, w_, kc_, wd_)
                    pi += 1

        em.wait("pool", dmaKST.sem, dmaKST.sem.n)
        em.wait("pool", dmaVST.sem, dmaVST.sem.n)
        em.op("pool", lambda e: e.collective_compute(
            "AllGather", ALU.bypass,
            replica_groups=[list(range(NCORES))],
            ins=[dContrib.opt()], outs=[dGath.opt()]), inc=CC)
        barrier()

    with contextlib.ExitStack() as es2:
        xnT_mine = es2.enter_context(nc.sbuf_tensor("xnTm", [128, DC * TC],
                                                    BF16))
        with contextlib.ExitStack() as es2a:
            xbf2 = [es2a.enter_context(nc.sbuf_tensor(f"x2bf{i}", [128, D],
                                                      BF16)) for i in range(2)]
            xnTm_ready = build_xnT(dXq, TC, xnT_mine, xbf2, [0, 0])
            barrier()

        h_half = es2.enter_context(nc.sbuf_tensor("h_half", [128, FFH * TC],
                                                  BF16))
        ffn_done = 0
        for half in range(2):
            with contextlib.ExitStack() as es2b:
                wg_buf = [es2b.enter_context(
                    nc.sbuf_tensor(f"wg{half}_{i}", [128, DC * 128], BF16))
                    for i in range(2)]
                wu_buf = [es2b.enter_context(
                    nc.sbuf_tensor(f"wu{half}_{i}", [128, DC * 128], BF16))
                    for i in range(2)]
                silu_t = [es2b.enter_context(
                    nc.sbuf_tensor(f"silu{half}_{i}", [128, TP], F32))
                    for i in range(2)]
                wg_free = [PE.n, PE.n]
                wu_free = [PE.n, PE.n]
                silu_free = [0, 0]
                gu_bank_free = [DVE.n] * 4
                for oo in range(FFH):
                    o = half * FFH + oo
                    wbi = oo % 2
                    gld = load_w_slice(dmaWG, wg_buf[wbi], dWgT, o * 128, 128,
                                       DC, waits=[(PE, wg_free[wbi])])
                    uld = load_w_slice(dmaWU, wu_buf[wbi], dWuT, o * 128, 128,
                                       DC, waits=[(PE, wu_free[wbi])])
                    em.wait("pe", dmaWG.sem, gld)
                    em.wait("pe", dmaWU.sem, uld)
                    if oo == 0 and half == 0:
                        em.wait("pe", ACT, xnTm_ready)
                    for b in range(4):
                        em.wait("pe", DVE, gu_bank_free[b])
                    for ic in range(DC):
                        for tp in range(NTP):
                            em.op("pe", lambda e, ic=ic, tp=tp, wbi=wbi,
                                  wg_buf=wg_buf: e.matmul(
                                      pb[tp % 2][:, 0:TP],
                                      wg_buf[wbi][:, ic * 128:ic * 128 + 128],
                                      xnT_mine[:, ic * TC + tp * TP:
                                               ic * TC + tp * TP + TP],
                                      start=(ic == 0), stop=(ic == DC - 1)),
                                  inc=PE)
                        for tp in range(NTP):
                            em.op("pe", lambda e, ic=ic, tp=tp, wbi=wbi,
                                  wu_buf=wu_buf: e.matmul(
                                      pb[2 + tp % 2][:, 0:TP],
                                      wu_buf[wbi][:, ic * 128:ic * 128 + 128],
                                      xnT_mine[:, ic * TC + tp * TP:
                                               ic * TC + tp * TP + TP],
                                      start=(ic == 0), stop=(ic == DC - 1)),
                                  inc=PE)
                    gu_m = PE.n
                    wg_free[wbi] = gu_m
                    wu_free[wbi] = gu_m
                    for tp in range(NTP):
                        stl = silu_t[tp % 2]
                        em.wait("act", PE, gu_m)
                        em.wait("act", DVE, silu_free[tp % 2])
                        em.op("act", lambda e, stl=stl, tp=tp: e.activation(
                            stl[:, 0:TP], pb[tp % 2][:, 0:TP], AF.Silu,
                            bias=0.0, scale=1.0), inc=ACT)
                        em.wait("dve", ACT, ACT.n)
                        em.op("dve", lambda e, stl=stl, tp=tp, oo=oo:
                              e.tensor_tensor(
                                  h_half[:, oo * TC + tp * TP:
                                         oo * TC + tp * TP + TP],
                                  stl[:, 0:TP], pb[2 + tp % 2][:, 0:TP],
                                  ALU.mult), inc=DVE)
                        silu_free[tp % 2] = DVE.n
                        gu_bank_free[tp % 2] = DVE.n
                        gu_bank_free[2 + tp % 2] = DVE.n
                    em.wait("pe", ACT, ACT.n)
                h_ready = DVE.n
                barrier()

            with contextlib.ExitStack() as es2c:
                wd_buf = [es2c.enter_context(
                    nc.sbuf_tensor(f"wd{half}_{i}", [128, FFH * 128], BF16))
                    for i in range(2)]
                ystage = [es2c.enter_context(
                    nc.sbuf_tensor(f"yst{half}_{i}", [128, TC], F32))
                    for i in range(2)]
                fpart_t = [es2c.enter_context(
                    nc.sbuf_tensor(f"fpt{half}_{i}", [128, TC], F32))
                    for i in range(2)]
                wd_free = [PE.n, PE.n]
                ystage_free = [0, 0]
                fp_free = [0, 0]
                dn_bank_free = {4: DVE.n, 5: DVE.n}
                for od in range(DC):
                    wbi = od % 2
                    wld = dmaWD.start(
                        lambda e, wbi=wbi, od=od, half=half, wd_buf=wd_buf:
                        e.dma_start(
                            out=wd_buf[wbi][:].rearrange(
                                "p (fc c) -> p fc c", c=128),
                            in_=dWdT[half * FFH * 128:(half + 1) * FFH * 128,
                                     od * 128:od * 128 + 128].rearrange(
                                         "(fc p) c -> p fc c", p=128)),
                        waits=[(PE, wd_free[wbi])])
                    em.wait("pe", dmaWD.sem, wld)
                    if od == 0:
                        em.wait("pe", DVE, h_ready)
                    for tp in range(NTP):
                        em.wait("pe", DVE, dn_bank_free[4 + tp % 2])
                    for fc in range(FFH):
                        for tp in range(NTP):
                            em.op("pe", lambda e, fc=fc, tp=tp, wbi=wbi,
                                  wd_buf=wd_buf: e.matmul(
                                      pb[4 + tp % 2][:, 0:TP],
                                      wd_buf[wbi][:, fc * 128:fc * 128 + 128],
                                      h_half[:, fc * TC + tp * TP:
                                             fc * TC + tp * TP + TP],
                                      start=(fc == 0), stop=(fc == FFH - 1)),
                                  inc=PE)
                    dn_m = PE.n
                    wd_free[wbi] = dn_m
                    ys = ystage[od % 2]
                    em.wait("dve", PE, dn_m)
                    em.wait("dve", dmaYST.sem, ystage_free[od % 2])
                    if half == 0:
                        for tp in range(NTP):
                            em.op("dve", lambda e, ys=ys, tp=tp: e.tensor_copy(
                                ys[:, tp * TP:tp * TP + TP],
                                pb[4 + tp % 2][:, 0:TP]), inc=DVE)
                        for tp in range(NTP):
                            dn_bank_free[4 + tp % 2] = DVE.n
                        ystage_free[od % 2] = dmaYST.start(
                            lambda e, ys=ys, od=od: e.dma_start(
                                out=dFpart[od * 128:od * 128 + 128, :],
                                in_=ys[:]), waits=[(DVE, DVE.n)])
                    else:
                        fp = fpart_t[od % 2]
                        fld = dmaFPL.start(lambda e, fp=fp, od=od: e.dma_start(
                            out=fp[:], in_=dFpart[od * 128:od * 128 + 128, :]),
                            waits=[(DVE, fp_free[od % 2])])
                        em.wait("dve", dmaFPL.sem, fld)
                        for tp in range(NTP):
                            em.op("dve", lambda e, ys=ys, fp=fp, tp=tp:
                                  e.tensor_tensor(
                                      ys[:, tp * TP:tp * TP + TP],
                                      fp[:, tp * TP:tp * TP + TP],
                                      pb[4 + tp % 2][:, 0:TP], ALU.add),
                                  inc=DVE)
                        for tp in range(NTP):
                            dn_bank_free[4 + tp % 2] = DVE.n
                        fp_free[od % 2] = DVE.n
                        ystage_free[od % 2] = dmaYST.start(
                            lambda e, ys=ys, od=od: e.dma_start(
                                out=dFfn[od * 128:od * 128 + 128, :],
                                in_=ys[:]), waits=[(DVE, DVE.n)])
                if half == 1:
                    ffn_done = dmaYST.sem.n
                barrier()

    es3 = contextlib.ExitStack()
    qrotT = es3.enter_context(nc.sbuf_tensor("qrotT", [128, NH * TC], BF16))
    with contextlib.ExitStack() as es3a:
        xnT3 = es3a.enter_context(nc.sbuf_tensor("xnT3", [128, DC * TC],
                                                 BF16))
        xbf3 = [es3a.enter_context(nc.sbuf_tensor(f"x3bf{i}", [128, D], BF16))
                for i in range(2)]
        xnT3_ready = build_xnT(dXq, TC, xnT3, xbf3, [0, 0])
        csq = es3a.enter_context(nc.sbuf_tensor("csq", [128, TC], F32))
        csq_ld = dmaC.start(lambda e: e.dma_start(out=csq[:], in_=dCq))
        wq_buf = [es3a.enter_context(
            nc.sbuf_tensor(f"wq{i}", [128, DC * 128], BF16)) for i in range(2)]
        qts = [[es3a.enter_context(
            nc.sbuf_tensor(f"qt{j}_{i}", [64, TP], F32)) for j in range(4)]
            for i in range(2)]
        qt_free = [0, 0]
        wq_free = [PE.n, PE.n]
        qbank_free = {0: DVE.n, 1: DVE.n, 2: DVE.n, 3: DVE.n}
        csq_waited = {"dve": False}
        for hh in range(NH):
            wbi = hh % 2
            wld = load_w_slice(dmaWG, wq_buf[wbi], dWq, hh * 128, 128, DC,
                               waits=[(PE, wq_free[wbi])])
            em.wait("pe", dmaWG.sem, wld)
            if hh == 0:
                em.wait("pe", ACT, xnT3_ready)
            qb0 = 2 * (hh % 2)
            for tp in range(NTP):
                em.wait("pe", DVE, qbank_free[qb0 + tp % 2])
            for ic in range(DC):
                for tp in range(NTP):
                    em.op("pe", lambda e, ic=ic, tp=tp, wbi=wbi, qb0=qb0:
                          e.matmul(
                              pb[qb0 + tp % 2][:, 0:TP],
                              wq_buf[wbi][:, ic * 128:ic * 128 + 128],
                              xnT3[:, ic * TC + tp * TP:
                                   ic * TC + tp * TP + TP],
                              start=(ic == 0), stop=(ic == DC - 1)), inc=PE)
            wq_free[wbi] = PE.n
            for tp in range(NTP):
                bank_i = qb0 + tp % 2
                bank = pb[bank_i]
                t1, t2, t3, t4 = qts[tp % 2]
                em.wait("dve", PE, PE.n)
                if not csq_waited["dve"]:
                    em.wait("dve", dmaC.sem, csq_ld)
                    csq_waited["dve"] = True
                em.wait("dve", DVE, qt_free[tp % 2])
                c0 = tp * TP
                qcol = hh * TC + tp * TP
                for dst, rlo, clo in ((t1, 0, 0), (t2, 64, 64),
                                      (t3, 0, 64), (t4, 64, 0)):
                    em.op("dve", lambda e, dst=dst, rlo=rlo, clo=clo,
                          bank=bank, c0=c0: e.tensor_tensor(
                              dst[:, 0:TP], bank[rlo:rlo + 64, 0:TP],
                              csq[clo:clo + 64, c0:c0 + TP], ALU.mult),
                          inc=DVE)
                em.wait("dve", DVE, DVE.n)
                em.op("dve", lambda e, t1=t1, t2=t2, qcol=qcol:
                      e.tensor_tensor(qrotT[0:64, qcol:qcol + TP],
                                      t1[:, 0:TP], t2[:, 0:TP],
                                      ALU.subtract), inc=DVE)
                em.op("dve", lambda e, t3=t3, t4=t4, qcol=qcol:
                      e.tensor_tensor(qrotT[64:128, qcol:qcol + TP],
                                      t3[:, 0:TP], t4[:, 0:TP], ALU.add),
                      inc=DVE)
                qt_free[tp % 2] = DVE.n
                qbank_free[bank_i] = DVE.n
        qrot_ready = DVE.n
        barrier()

    with contextlib.ExitStack() as es4:
        NTB = T // 128
        ksb2 = [es4.enter_context(nc.sbuf_tensor(f"ksb{i}", [128, T], BF16))
                for i in range(2)]
        vsb2 = [es4.enter_context(nc.sbuf_tensor(f"vsb{i}", [128, NTB * 129],
                                                 BF16)) for i in range(2)]
        NB_P = 6
        psb = es4.enter_context(nc.sbuf_tensor("psb", [128, NB_P * 512],
                                               BF16))
        attst = [es4.enter_context(nc.sbuf_tensor(f"atts{i}", [128, G * 128],
                                                  BF16)) for i in range(2)]
        rs_t = [es4.enter_context(nc.sbuf_tensor(f"rs{i}", [128, 1], F32))
                for i in range(2)]

        for i in range(2):
            em.op("pool", lambda e, i=i: e.memset(
                vsb2[i][:].rearrange("p (tb e) -> p tb e", e=129)[
                    :, :, 128:129], 1.0), inc=POOL)
        ones_ready = POOL.n

        ST_B = (0, 1, 2)
        AV_B = (3, 4)
        st_bank_free = {b: 0 for b in ST_B}
        psb_slot_free = [0] * NB_P
        av_bank_free = {b: DVE.n for b in AV_B}
        exp_marks = {}
        att_free_t = [0, 0]
        kv_loaded = [0, 0]
        kv_free = [PE.n, PE.n]
        QR = DKV + SPAN

        def issue_kv_load(j):
            par = j % 2
            w0 = [(PE, kv_free[par])]
            if j == 0:
                w0.append((CC, 1))
            dmaKVL.start(lambda e, par=par, j=j: e.dma_start(
                out=ksb2[par][:].rearrange("p (r c) -> p r c", c=SPAN),
                in_=dGath.rearrange("(r q) c -> q r c", q=QR)[
                    j * 128:j * 128 + 128]), waits=w0)
            for r in range(NCORES):
                dmaKVL.start(lambda e, par=par, j=j, r=r: e.dma_start(
                    out=vsb2[par][:].rearrange("p (tb e) -> p tb e", e=129)[
                        :, r * (SPAN // 128):(r + 1) * (SPAN // 128), 0:128],
                    in_=dGath[r * QR + DKV:(r + 1) * QR,
                              j * 128:j * 128 + 128].rearrange(
                        "(jj p) c -> p jj c", p=128)))
            kv_loaded[par] = dmaKVL.sem.n

        gi_n = [0]
        avq = []
        LAG = 2

        def emit_st_grp(h, g, t, gr, ngr):
            gi = gi_n[0]
            gi_n[0] += 1
            par = h % 2
            bank_i = ST_B[gi % 3]
            bank = pb[bank_i]
            slot = gi % NB_P
            qcol = (h * G + g) * TC + t * 128
            diag = (gr >= ngr - 2)
            em.wait("pe", ACT, st_bank_free[bank_i])
            for j in range(4):
                kk = gr * 4 + j
                em.op("pe", lambda e, j=j, kk=kk, bank=bank, par=par,
                      qcol=qcol, diag=diag: e.matmul(
                          bank[:, 128 * j:128 * j + 128],
                          ksb2[par][:, 128 * kk:128 * kk + 128],
                          qrotT[:, qcol:qcol + 128],
                          start=True, stop=not diag), inc=PE)
                if diag:
                    moff = (kk - 8 * t) * 128
                    em.op("pe", lambda e, j=j, bank=bank, moff=moff:
                          e.matmul(bank[:, 128 * j:128 * j + 128],
                                   ident[:], maskm[:, moff:moff + 128],
                                   start=False, stop=True), inc=PE)
            em.wait("act", PE, PE.n)
            em.wait("act", PE, psb_slot_free[slot])
            em.op("act", lambda e, slot=slot, bank=bank: e.activation(
                psb[:, slot * 512:slot * 512 + 512], bank[:],
                AF.Exp, bias=0.0, scale=1.0), inc=ACT)
            exp_marks[gi] = ACT.n
            st_bank_free[bank_i] = ACT.n
            return gi

        def emit_av_grp(h, g, t, gr, S, gi, si):
            par = h % 2
            avb_i = AV_B[si % 2]
            avb = pb[avb_i]
            slot = gi % NB_P
            em.wait("pe", ACT, exp_marks[gi])
            if gr == 0:
                em.wait("pe", DVE, av_bank_free[avb_i])
                em.wait("pe", POOL, ones_ready)
            for j in range(4):
                kk = gr * 4 + j
                em.op("pe", lambda e, j=j, kk=kk, avb=avb, par=par,
                      slot=slot, S=S: e.matmul(
                          avb[:, 0:129],
                          psb[:, slot * 512 + 128 * j:
                              slot * 512 + 128 * j + 128],
                          vsb2[par][:, 129 * kk:129 * kk + 129],
                          start=(kk == 0), stop=(kk == S - 1)), inc=PE)
            psb_slot_free[slot] = PE.n
            if gr == S // 4 - 1:
                pqr = si % 2
                em.wait("dve", PE, PE.n)
                em.wait("dve", DVE, DVE.n)
                em.op("dve", lambda e, pqr=pqr, avb=avb: e.reciprocal(
                    rs_t[pqr][:], avb[:, 128:129]), inc=DVE)
                em.wait("dve", DVE, DVE.n)
                em.wait("dve", dmaATT.sem, att_free_t[t % 2])
                em.op("dve", lambda e, g=g, t=t, pqr=pqr, avb=avb:
                      e.tensor_scalar(
                          attst[t % 2][:, g * 128:g * 128 + 128],
                          avb[:, 0:128], rs_t[pqr][:], None, ALU.mult),
                      inc=DVE)
                av_bank_free[avb_i] = DVE.n
                if g == G - 1:
                    att_free_t[t % 2] = dmaATT.start(
                        lambda e, h=h, t=t: e.dma_start(
                            out=dAtt[t * 128:t * 128 + 128,
                                     h * G * 128:(h + 1) * G * 128],
                            in_=attst[t % 2][:]),
                        waits=[(DVE, DVE.n)])

        def pump(limit):
            while len(avq) > limit:
                item = avq.pop(0)
                emit_av_grp(*item[:7])
                if item[7]:
                    kv_free[item[0] % 2] = PE.n

        issue_kv_load(0)
        si_n = [0]
        for h in range(NKV):
            em.wait("pe", dmaKVL.sem, kv_loaded[h % 2])
            if h == 0:
                em.wait("pe", DVE, qrot_ready)
            for t in range(NQB):
                for g in range(G):
                    si = si_n[0]
                    si_n[0] += 1
                    S = 8 * t + 8
                    for gr in range(S // 4):
                        gi = emit_st_grp(h, g, t, gr, S // 4)
                        head_last = (t == NQB - 1 and g == G - 1
                                     and gr == S // 4 - 1)
                        avq.append((h, g, t, gr, S, gi, si, head_last))
                        pump(LAG)
                if t == 0 and h + 1 < NKV:
                    pump(0)
                    issue_kv_load(h + 1)
        pump(0)
        barrier()
    es3.close()
    barrier()

    with contextlib.ExitStack() as es5:
        attT = es5.enter_context(nc.sbuf_tensor("attT", [128, DC * TC], BF16))
        wo_buf = [es5.enter_context(
            nc.sbuf_tensor(f"wo{i}", [128, DC * 128], BF16)) for i in range(2)]
        xr_t = [es5.enter_context(nc.sbuf_tensor(f"xr{i}", [128, TC], F32))
                for i in range(2)]
        fr_t = [es5.enter_context(nc.sbuf_tensor(f"fr{i}", [128, TC], F32))
                for i in range(2)]
        o1_t = [es5.enter_context(nc.sbuf_tensor(f"o1{i}", [128, TC], F32))
                for i in range(2)]

        att_tile = [es5.enter_context(nc.sbuf_tensor(f"at5_{i}", [128, D],
                                                     BF16)) for i in range(2)]
        at_free = [PE.n, PE.n]
        attT_ready = 0
        for tt in range(TC // 128):
            bi = tt % 2
            ld = dmaATT.start(lambda e, tt=tt, bi=bi: e.dma_start(
                out=att_tile[bi][:], in_=dAtt[tt * 128:tt * 128 + 128, :]),
                waits=[(PE, at_free[bi])])
            em.wait("pe", dmaATT.sem, ld)
            attT_ready, pem = transpose_tiles(att_tile[bi], attT, TC,
                                              tt * 128, 0, DC)
            at_free[bi] = pem
        wo_free = [PE.n, PE.n]
        obank_free = {0: DVE.n, 1: DVE.n}
        res_marks = [0, 0]
        out_marks = [0, 0]
        for od in range(DC):
            wbi = od % 2
            wld = load_w_slice(dmaWG, wo_buf[wbi], dWo, od * 128, 128, DC,
                               waits=[(PE, wo_free[wbi])])
            xr, fr, o1 = xr_t[od % 2], fr_t[od % 2], o1_t[od % 2]
            xld = dmaXR.start(lambda e, xr=xr, od=od: e.dma_start(
                out=xr[:], in_=dXqT[od * 128:od * 128 + 128, :]),
                waits=[(DVE, res_marks[od % 2])])
            fld = dmaFR.start(lambda e, fr=fr, od=od: e.dma_start(
                out=fr[:], in_=dFfn[od * 128:od * 128 + 128, :]),
                waits=([(dmaYST.sem, ffn_done), (DVE, res_marks[od % 2])]
                       if od == 0 else [(DVE, res_marks[od % 2])]))
            em.wait("pe", dmaWG.sem, wld)
            if od == 0:
                em.wait("pe", ACT, attT_ready)
            for tp in range(NTP):
                em.wait("pe", DVE, obank_free[tp % 2])
            for ic in range(DC):
                for tp in range(NTP):
                    em.op("pe", lambda e, ic=ic, tp=tp, wbi=wbi: e.matmul(
                        pb[tp % 2][:, 0:TP],
                        wo_buf[wbi][:, ic * 128:ic * 128 + 128],
                        attT[:, ic * TC + tp * TP: ic * TC + tp * TP + TP],
                        start=(ic == 0), stop=(ic == DC - 1)), inc=PE)
            wo_free[wbi] = PE.n
            em.wait("dve", PE, PE.n)
            em.wait("dve", dmaXR.sem, xld)
            em.wait("dve", dmaFR.sem, fld)
            em.wait("dve", dmaOUT.sem, out_marks[od % 2])
            for tp in range(NTP):
                em.op("dve", lambda e, o1=o1, xr=xr, tp=tp: e.tensor_tensor(
                    o1[:, tp * TP:tp * TP + TP], xr[:, tp * TP:tp * TP + TP],
                    pb[tp % 2][:, 0:TP], ALU.add), inc=DVE)
                obank_free[tp % 2] = DVE.n
            em.wait("dve", DVE, DVE.n)
            em.op("dve", lambda e, o1=o1, fr=fr: e.tensor_tensor(
                o1[:], o1[:], fr[:], ALU.add), inc=DVE)
            res_marks[od % 2] = DVE.n
            out_marks[od % 2] = dmaOUT.start(
                lambda e, o1=o1, od=od: e.dma_start(
                    out=dOut[od * 128:od * 128 + 128, :], in_=o1[:]),
                waits=[(DVE, DVE.n)])
        em.wait("sp", dmaOUT.sem, dmaOUT.sem.n)

    em.flush(nc)
    es.close()
    return nc



def _host_inputs(cfg, core, X, Wq, Wk, Wv, Wo, Wg, Wu, Wd):
    T = cfg.T
    f32 = np.float32
    bf16 = ml_dtypes.bfloat16
    Xq = np.ascontiguousarray(X[core::NCORES])
    XqT = np.ascontiguousarray(Xq.T)
    Xkv = np.ascontiguousarray(X[core * SPAN:(core + 1) * SPAN])

    half = DH // 2
    inv = 1.0 / (10000.0 ** (np.arange(half, dtype=np.float64) / half))
    pos_blk = np.arange(core * SPAN, (core + 1) * SPAN, dtype=np.float64)
    ang = pos_blk[None, :] * inv[:, None]
    cs_kv = np.concatenate([np.cos(ang), np.sin(ang)], axis=0).astype(f32)
    pos_q = np.arange(T, dtype=np.float64)[core::NCORES]
    angq = pos_q[None, :] * inv[:, None]
    s = 1.0 / np.sqrt(np.float64(DH))
    cs_q = np.concatenate([np.cos(angq) * s, np.sin(angq) * s],
                          axis=0).astype(f32)

    i_ = np.arange(QBLK)[:, None]
    rj = np.arange(SPAN)[None, :]
    mp = (128 * (rj // 128) + i_) <= (NCORES * (rj % 128) + core)
    maskm = np.where(mp, 0.0, -30000.0).astype(f32).astype(bf16)
    ident = np.eye(128, dtype=f32).astype(bf16)

    return {
        "Xkv": Xkv, "Xq": Xq, "XqT": XqT,
        "Wq": Wq, "Wk": Wk, "Wv": Wv, "Wo": Wo,
        "WgT": np.ascontiguousarray(Wg.T),
        "WuT": np.ascontiguousarray(Wu.T),
        "WdT": np.ascontiguousarray(Wd.T),
        "cs_kv": cs_kv, "cs_q": cs_q, "maskm": maskm, "ident": ident,
    }


def run(cfg, X, Wq, Wk, Wv, Wo, Wg, Wu, Wd, trace=False):
    nc = build_program(cfg)
    in_maps = [_host_inputs(cfg, c, X, Wq, Wk, Wv, Wo, Wg, Wu, Wd)
               for c in range(NCORES)]
    res = run_bass_kernel_spmd(nc, in_maps, core_ids=list(range(NCORES)),
                               trace=trace)
    out = np.empty((cfg.T, cfg.D), np.float32)
    for c in range(NCORES):
        out[c::NCORES, :] = res.results[c]["outT"].T
    return out, res


def kernel(L0_attn_X, L0_attn_Wq, L0_attn_Wk, L0_attn_Wv, L0_attn_Wo,
           L0_ffn_W_gate, L0_ffn_W_up, L0_ffn_W_down):
    cfg = Cfg()
    X = np.ascontiguousarray(np.asarray(L0_attn_X, np.float32).reshape(
        cfg.T, cfg.D))
    out, _ = run(cfg, X,
                 np.asarray(L0_attn_Wq, np.float32),
                 np.asarray(L0_attn_Wk, np.float32),
                 np.asarray(L0_attn_Wv, np.float32),
                 np.asarray(L0_attn_Wo, np.float32),
                 np.asarray(L0_ffn_W_gate, np.float32),
                 np.asarray(L0_ffn_W_up, np.float32),
                 np.asarray(L0_ffn_W_down, np.float32))
    return out.reshape(1, cfg.T, cfg.D)


# version 10 (final)
# speedup vs baseline: 1.7401x; 1.0040x over previous
import contextlib
import numpy as np
import ml_dtypes
import concourse.bass as bass
import concourse.mybir as mybir
from concourse.bass_utils import run_bass_kernel_spmd

F32 = mybir.dt.float32
BF16 = mybir.dt.bfloat16
AF = mybir.ActivationFunctionType
ALU = mybir.AluOpType
AX = mybir.AxisListType

NCORES = 8
DH = 128
QBLK = 128
SPAN = NCORES * QBLK


class Cfg:
    def __init__(self, T=8192, D=4096, DFF=11008, NH=32, NKV=8):
        self.T, self.D, self.DFF, self.NH, self.NKV = T, D, DFF, NH, NKV
        self.TC = T // NCORES
        self.NQB = self.TC // QBLK
        self.DC = D // 128
        self.DKV = NKV * DH
        self.KVC = self.DKV // 128
        self.FFC = DFF // 128
        self.FFH = self.FFC // 2
        self.G = NH // NKV
        self.NT = T // 128
        assert self.TC % QBLK == 0 and self.FFC % 2 == 0 and D % 128 == 0
        assert T % SPAN == 0 and NH % NKV == 0 and self.NT % 4 == 0


class Sem:
    def __init__(self, es, nc, name, step=1):
        self.h = es.enter_context(nc.semaphore(name))
        self.n = 0
        self.step = step


class Emitter:
    ENGINES = ("pe", "dve", "act", "pool", "sp")

    def __init__(self):
        self.ops = {e: [] for e in self.ENGINES}

    def op(self, eng, fn, inc=None, k=1):
        if inc is not None:
            def wrapped(e, fn=fn, s=inc, k=k):
                fn(e).then_inc(s.h, k * s.step)
            self.ops[eng].append(wrapped)
            inc.n += k * inc.step
            return inc.n
        self.ops[eng].append(fn)
        return None

    def wait(self, eng, sem, val):
        if val > 0:
            self.ops[eng].append(lambda e, h=sem.h, v=val: e.wait_ge(h, v))

    def flush(self, nc):
        with nc.Block() as block:
            @block.tensor
            def _(t):
                for f in self.ops["pe"]:
                    f(t)

            @block.vector
            def _(v):
                for f in self.ops["dve"]:
                    f(v)

            @block.scalar
            def _(s):
                for f in self.ops["act"]:
                    f(s)

            @block.gpsimd
            def _(g):
                for f in self.ops["pool"]:
                    f(g)

            @block.sync
            def _(sy):
                for f in self.ops["sp"]:
                    f(sy)


class Dma:

    def __init__(self, es, nc, em, name, queue):
        self.sem = Sem(es, nc, name, step=16)
        self.em = em
        self.q = queue

    def start(self, fn, waits=()):
        em = self.em
        em.wait(self.q, self.sem, self.sem.n)
        for (s, v) in waits:
            em.wait(self.q, s, v)
        em.op(self.q, fn, inc=self.sem)
        return self.sem.n


def build_program(cfg):
    nc = bass.Bass("TRN2", target_bir_lowering=False, debug=False)
    T, D, DFF, NH, NKV = cfg.T, cfg.D, cfg.DFF, cfg.NH, cfg.NKV
    TC, DC, DKV, KVC, FFH, G = (cfg.TC, cfg.DC, cfg.DKV, cfg.KVC,
                                cfg.FFH, cfg.G)
    NQB, NT = cfg.NQB, cfg.NT
    TP = min(TC, 512)
    NTP = TC // TP

    dXkv = nc.dram_tensor("Xkv", [SPAN, D], F32, kind="ExternalInput").ap()
    dXq = nc.dram_tensor("Xq", [TC, D], F32, kind="ExternalInput").ap()
    dXqT = nc.dram_tensor("XqT", [D, TC], F32, kind="ExternalInput").ap()
    dWq = nc.dram_tensor("Wq", [D, D], F32, kind="ExternalInput").ap()
    dWk = nc.dram_tensor("Wk", [D, DKV], F32, kind="ExternalInput").ap()
    dWv = nc.dram_tensor("Wv", [D, DKV], F32, kind="ExternalInput").ap()
    dWo = nc.dram_tensor("Wo", [D, D], F32, kind="ExternalInput").ap()
    dWgT = nc.dram_tensor("WgT", [D, DFF], F32, kind="ExternalInput").ap()
    dWuT = nc.dram_tensor("WuT", [D, DFF], F32, kind="ExternalInput").ap()
    dWdT = nc.dram_tensor("WdT", [DFF, D], F32, kind="ExternalInput").ap()
    dCkv = nc.dram_tensor("cs_kv", [128, SPAN], F32, kind="ExternalInput").ap()
    dCq = nc.dram_tensor("cs_q", [128, TC], F32, kind="ExternalInput").ap()
    dMask = nc.dram_tensor("maskm", [128, SPAN], BF16, kind="ExternalInput").ap()
    dIdent = nc.dram_tensor("ident", [128, 128], BF16, kind="ExternalInput").ap()
    dOut = nc.dram_tensor("outT", [D, TC], F32, kind="ExternalOutput").ap()

    dContribT = nc.dram_tensor("kvcontrib", [DKV + SPAN, SPAN], BF16,
                               kind="Internal")
    dGathT = nc.dram_tensor("kvgath", [NCORES * (DKV + SPAN), SPAN], BF16,
                            kind="Internal")
    dContrib = dContribT.ap()
    dGath = dGathT.ap()
    dQrot = nc.dram_tensor("qrotscr", [128, NH * TC], BF16,
                           kind="Internal").ap()
    dFpart = nc.dram_tensor("ffnpart", [D, TC], F32, kind="Internal").ap()
    dFfn = nc.dram_tensor("ffnscr", [D, TC], F32, kind="Internal").ap()
    dAtt = nc.dram_tensor("attscr", [TC, D], BF16, kind="Internal").ap()

    es = contextlib.ExitStack()
    em = Emitter()

    PE = Sem(es, nc, "s_pe")
    DVE = Sem(es, nc, "s_dve")
    ACT = Sem(es, nc, "s_act")
    POOL = Sem(es, nc, "s_pool")
    CC = Sem(es, nc, "s_cc", step=1)

    dmaC = Dma(es, nc, em, "d_const", "sp")
    dmaX = Dma(es, nc, em, "d_x", "pool")
    dmaWKV = Dma(es, nc, em, "d_wkv", "pool")
    dmaWG = Dma(es, nc, em, "d_wg", "pool")
    dmaWU = Dma(es, nc, em, "d_wu", "pool")
    dmaWD = Dma(es, nc, em, "d_wd", "pool")
    dmaKST = Dma(es, nc, em, "d_kst", "sp")
    dmaVST = Dma(es, nc, em, "d_vst", "sp")
    dmaYST = Dma(es, nc, em, "d_yst", "sp")
    dmaQS = Dma(es, nc, em, "d_qs", "sp")
    dmaFPL = Dma(es, nc, em, "d_fpl", "sp")
    dmaKVL = Dma(es, nc, em, "d_kvl", "pool")
    dmaATT = Dma(es, nc, em, "d_att", "sp")
    dmaXR = Dma(es, nc, em, "d_xr", "sp")
    dmaFR = Dma(es, nc, em, "d_fr", "sp")
    dmaOUT = Dma(es, nc, em, "d_out", "sp")
    all_sems = [PE, DVE, ACT, POOL] + [d.sem for d in (
        dmaC, dmaX, dmaWKV, dmaWG, dmaWU, dmaWD, dmaKST, dmaVST, dmaYST,
        dmaQS, dmaFPL, dmaKVL, dmaATT, dmaXR, dmaFR, dmaOUT)]

    def barrier():
        for eng in em.ENGINES:
            for s in all_sems:
                em.wait(eng, s, s.n)

    pb = [es.enter_context(nc.psum_tensor(f"pb{i}", [128, 512], F32))
          for i in range(8)]

    ident = es.enter_context(nc.sbuf_tensor("ident_sb", [128, 128], BF16))
    maskm = es.enter_context(nc.sbuf_tensor("mask_sb", [128, SPAN], BF16))
    eps_t = es.enter_context(nc.sbuf_tensor("eps_sb", [128, 1], F32))
    st = {
        "bn": es.enter_context(nc.sbuf_tensor("st_bn", [128, 6 * (D // 512)],
                                              F32)),
        "ag": es.enter_context(nc.sbuf_tensor("st_ag", [128, 2], F32)),
        "sq": es.enter_context(nc.sbuf_tensor("st_sq", [128, 1], F32)),
        "rs": es.enter_context(nc.sbuf_tensor("st_rs", [128, 1], F32)),
    }

    dmaC.start(lambda e: e.dma_start(out=ident[:], in_=dIdent))
    const_ld = dmaC.start(lambda e: e.dma_start(out=maskm[:], in_=dMask))
    em.op("pool", lambda e: e.memset(eps_t[:], 1e-5), inc=POOL)
    eps_ready = POOL.n
    em.wait("pe", dmaC.sem, const_ld)
    em.wait("dve", dmaC.sem, const_ld)

    ln_state = {"first": True}

    def ln_tile(xsrc, row0, buf, free_mark):
        dmaX.start(
            lambda e: e.dma_start(out=buf[:], in_=xsrc[row0:row0 + 128, :]),
            waits=[(PE, free_mark)])
        em.wait("dve", dmaX.sem, dmaX.sem.n)
        nblk = D // 512
        for b in range(nblk):
            em.op("dve", lambda e, b=b, buf=buf: e.bn_stats(
                st["bn"][:, 6 * b:6 * b + 6], buf[:, 512 * b:512 * b + 512]),
                inc=DVE)
        em.wait("dve", DVE, DVE.n)
        em.op("dve", lambda e: e.bn_aggr(st["ag"][:], st["bn"][:, 0:6 * nblk]),
              inc=DVE)
        em.wait("act", DVE, DVE.n)
        if ln_state["first"]:
            em.wait("act", POOL, eps_ready)
            ln_state["first"] = False
        em.op("act", lambda e: e.activation(st["sq"][:], st["ag"][:, 1:2],
                                            AF.Sqrt, bias=eps_t[:], scale=1.0),
              inc=ACT)
        em.wait("dve", ACT, ACT.n)
        em.op("dve", lambda e: e.reciprocal(st["rs"][:], st["sq"][:]), inc=DVE)
        em.wait("dve", DVE, DVE.n)
        em.op("dve", lambda e, buf=buf: e.tensor_scalar(
            buf[:], buf[:], st["ag"][:, 0:1], st["rs"][:],
            ALU.subtract, ALU.mult), inc=DVE)
        return DVE.n

    tb_free = {6: 0, 7: 0}

    def transpose_tiles(src, dst, ncols, col0, ln_mark, nch_total):
        em.wait("pe", DVE, ln_mark)
        dview = dst[:].rearrange("p (ch c) -> p ch c", c=ncols)
        for q4 in range((nch_total + 3) // 4):
            bank_i = 6 + q4 % 2
            bank = pb[bank_i]
            nch = min(4, nch_total - 4 * q4)
            em.wait("pe", ACT, tb_free[bank_i])
            for j in range(nch):
                ch = 4 * q4 + j
                em.op("pe", lambda e, ch=ch, bank=bank, j=j, src=src: e.matmul(
                    bank[:, 128 * j:128 * j + 128],
                    src[:, 128 * ch:128 * ch + 128], ident[:],
                    start=True, stop=True), inc=PE)
            em.wait("act", PE, PE.n)
            em.op("act", lambda e, bank=bank, q4=q4, nch=nch, dview=dview:
                  e.copy(dview[:, 4 * q4:4 * q4 + nch, col0:col0 + 128],
                         bank[:].rearrange("p (j c) -> p j c", c=128)[:, 0:nch, :]),
                  inc=ACT)
            tb_free[bank_i] = ACT.n
        return ACT.n, PE.n

    def load_w_slice(dma, dst, wdram, col0, ncols, nch, waits=()):
        return dma.start(lambda e: e.dma_start(
            out=dst[:, 0:nch * ncols].rearrange("p (ic c) -> p ic c", c=ncols),
            in_=wdram[0:nch * 128, col0:col0 + ncols].rearrange(
                "(ic p) c -> p ic c", p=128)), waits=waits)

    def build_xnT(xsrc, ntok, xnT, xbf_bufs, xbf_free):
        m = 0
        for tt in range(ntok // 128):
            bi = tt % 2
            lnm = ln_tile(xsrc, tt * 128, xbf_bufs[bi], xbf_free[bi])
            m, pem = transpose_tiles(xbf_bufs[bi], xnT, ntok, tt * 128,
                                     lnm, DC)
            xbf_free[bi] = pem
        return m

    BS = SPAN
    with contextlib.ExitStack() as es1:
        xnT2 = [es1.enter_context(nc.sbuf_tensor(f"p1xnT{i}", [128, DC * BS],
                                                 BF16)) for i in range(2)]
        xbf = [es1.enter_context(nc.sbuf_tensor(f"xbf{i}", [128, D], BF16))
               for i in range(2)]
        wkv_buf = [es1.enter_context(
            nc.sbuf_tensor(f"wkv{i}", [128, DC * 128], BF16)) for i in range(2)]
        csb = [es1.enter_context(nc.sbuf_tensor(f"csb{i}", [128, BS], F32))
               for i in range(2)]
        krot = [es1.enter_context(nc.sbuf_tensor(f"krot{i}", [128, 512], BF16))
                for i in range(2)]
        kts = [[es1.enter_context(nc.sbuf_tensor(f"kt{j}_{i}", [64, 512], F32))
                for j in range(4)] for i in range(2)]
        vstage = [es1.enter_context(
            nc.sbuf_tensor(f"vst{i}", [128, 512], BF16)) for i in range(2)]
        vstage2 = [es1.enter_context(
            nc.sbuf_tensor(f"vst2_{i}", [128, 512], BF16)) for i in range(2)]

        NBLK = SPAN // BS
        NPIECE = BS // 512
        NTILE = BS // 128
        xbf_free = [0, 0]
        wkv_free = [0, 0]
        kbank_free = {0: 0, 2: 0, 4: 0}
        vbank_free = {1: 0, 3: 0, 5: 0}
        krot_free = [0, 0]
        vst_free = [0, 0]
        vst2_free = [0, 0]
        kt_free = [0, 0]
        cs_marks = [0, 0]
        cs_used = [0, 0]
        cskv_waited = [False, False]
        vpi = 0
        xnT_ready = [0, 0]

        def emit_ln_tile(blk, tt):
            bi = tt % 2
            lnm = ln_tile(dXkv, blk * BS + tt * 128, xbf[bi], xbf_free[bi])
            m, pem = transpose_tiles(xbf[bi], xnT2[blk % 2], BS, tt * 128,
                                     lnm, DC)
            xbf_free[bi] = pem
            if tt == NTILE - 1:
                xnT_ready[blk % 2] = m

        def emit_proj(blk, which, kc, wdram):
            nonlocal vpi
            xnT_blk = xnT2[blk % 2]
            idx = which * KVC + kc
            wbi = idx % 2
            wld = load_w_slice(dmaWKV, wkv_buf[wbi], wdram, kc * 128,
                               128, DC, waits=[(PE, wkv_free[wbi])])
            em.wait("pe", dmaWKV.sem, wld)
            em.wait("pe", ACT, xnT_ready[blk % 2])
            base = 0 if which == 0 else 1
            bis = [base + 2 * ((kc * NPIECE + piece) % 3)
                   for piece in range(NPIECE)]
            banks = [pb[b] for b in bis]
            for b in bis:
                em.wait("pe", DVE,
                        (kbank_free if which == 0 else vbank_free)[b])
            for ic in range(DC):
                for piece in range(NPIECE):
                    em.op("pe", lambda e, ic=ic, wbi=wbi,
                          piece=piece, bank=banks[piece], xnT_blk=xnT_blk:
                          e.matmul(
                              bank[:],
                              wkv_buf[wbi][:, ic * 128:ic * 128 + 128],
                              xnT_blk[:, ic * BS + piece * 512:
                                      ic * BS + piece * 512 + 512],
                              start=(ic == 0), stop=(ic == DC - 1)),
                          inc=PE)
            wkv_free[wbi] = PE.n
            for piece in range(NPIECE):
                bank_i = bis[piece]
                bank = pb[bank_i]
                projm = PE.n
                c0 = blk * BS + piece * 512
                cloc = piece * 512
                cbi = blk % 2
                if which == 0:
                    par = (kc * NPIECE + piece) % 2
                    kb = krot[par]
                    t1, t2, t3, t4 = kts[par]
                    em.wait("dve", PE, projm)
                    if not cskv_waited[cbi]:
                        em.wait("dve", dmaC.sem, cs_marks[cbi])
                        cskv_waited[cbi] = True
                    em.wait("dve", dmaKST.sem, krot_free[par])
                    em.wait("dve", DVE, kt_free[par])
                    for dst, rlo, clo in ((t1, 0, 0), (t2, 64, 64),
                                          (t3, 0, 64), (t4, 64, 0)):
                        em.op("dve", lambda e, dst=dst, rlo=rlo,
                              clo=clo, bank=bank, cloc=cloc, cbi=cbi:
                              e.tensor_tensor(
                                  dst[:], bank[rlo:rlo + 64, :],
                                  csb[cbi][clo:clo + 64, cloc:cloc + 512],
                                  ALU.mult), inc=DVE)
                    em.wait("dve", DVE, DVE.n)
                    em.op("dve", lambda e, kb=kb, t1=t1, t2=t2:
                          e.tensor_tensor(kb[0:64, :], t1[:], t2[:],
                                          ALU.subtract), inc=DVE)
                    em.op("dve", lambda e, kb=kb, t3=t3, t4=t4:
                          e.tensor_tensor(kb[64:128, :], t3[:], t4[:],
                                          ALU.add), inc=DVE)
                    kt_free[par] = DVE.n
                    kbank_free[bank_i] = DVE.n
                    cs_used[cbi] = DVE.n
                    krot_free[par] = dmaKST.start(
                        lambda e, kb=kb, kc=kc, c0=c0: e.dma_start(
                            out=dContrib[kc * 128:kc * 128 + 128,
                                         c0:c0 + 512],
                            in_=kb[:]), waits=[(DVE, DVE.n)])
                else:
                    par = vpi % 2
                    vb = vstage[par]
                    vb2 = vstage2[par]
                    em.wait("dve", PE, projm)
                    em.wait("dve", PE, vst_free[par])
                    em.op("dve", lambda e, vb=vb, bank=bank:
                          e.tensor_copy(vb[:], bank[:]), inc=DVE)
                    vbank_free[bank_i] = DVE.n
                    vevac = DVE.n
                    tbank_i = 6 + vpi % 2
                    tbank = pb[tbank_i]
                    em.wait("pe", DVE, vevac)
                    em.wait("pe", ACT, tb_free[tbank_i])
                    for j in range(4):
                        em.op("pe", lambda e, j=j, vb=vb, tbank=tbank:
                              e.matmul(
                                  tbank[:, 128 * j:128 * j + 128],
                                  vb[:, 128 * j:128 * j + 128],
                                  ident[:], start=True, stop=True),
                              inc=PE)
                    vst_free[par] = PE.n
                    em.wait("act", PE, PE.n)
                    em.wait("act", dmaVST.sem, vst2_free[par])
                    em.op("act", lambda e, vb2=vb2, tbank=tbank:
                          e.copy(vb2[:], tbank[:]), inc=ACT)
                    tb_free[tbank_i] = ACT.n
                    jj0 = c0 // 128
                    vst2_free[par] = dmaVST.start(
                        lambda e, vb2=vb2, kc=kc, jj0=jj0:
                        e.dma_start(
                            out=dContrib[DKV:DKV + SPAN, :].rearrange(
                                "(jj p) c -> p jj c", p=128)[
                                :, jj0:jj0 + 4, kc * 128:kc * 128 + 128],
                            in_=vb2[:].rearrange("p (j d) -> p j d", d=128)),
                        waits=[(ACT, ACT.n)])
                    vpi += 1

        projs = [(0, kc, dWk) for kc in range(KVC)] + \
                [(1, kc, dWv) for kc in range(KVC)]
        PPT = max(1, (2 * KVC) // NTILE)
        for blk in range(NBLK + 1):
            if blk < NBLK:
                em.wait("sp", DVE, cs_used[blk % 2])
                cs_marks[blk % 2] = dmaC.start(
                    lambda e, blk=blk: e.dma_start(
                        out=csb[blk % 2][:],
                        in_=dCkv[:, blk * BS:(blk + 1) * BS]))
                cskv_waited[blk % 2] = False
            pi = 0
            for tt in range(NTILE):
                if blk >= 1:
                    for _ in range(PPT):
                        if pi < len(projs):
                            w_, kc_, wd_ = projs[pi]
                            emit_proj(blk - 1, w_, kc_, wd_)
                            pi += 1
                if blk < NBLK:
                    emit_ln_tile(blk, tt)
            if blk >= 1:
                while pi < len(projs):
                    w_, kc_, wd_ = projs[pi]
                    emit_proj(blk - 1, w_, kc_, wd_)
                    pi += 1

        em.wait("pool", dmaKST.sem, dmaKST.sem.n)
        em.wait("pool", dmaVST.sem, dmaVST.sem.n)
        em.op("pool", lambda e: e.collective_compute(
            "AllGather", ALU.bypass,
            replica_groups=[list(range(NCORES))],
            ins=[dContrib.opt()], outs=[dGath.opt()]), inc=CC)
        barrier()

    with contextlib.ExitStack() as es2:
        xnT_mine = es2.enter_context(nc.sbuf_tensor("xnTm", [128, DC * TC],
                                                    BF16))
        with contextlib.ExitStack() as es2a:
            xbf2 = [es2a.enter_context(nc.sbuf_tensor(f"x2bf{i}", [128, D],
                                                      BF16)) for i in range(2)]
            xnTm_ready = build_xnT(dXq, TC, xnT_mine, xbf2, [0, 0])
            barrier()
        with contextlib.ExitStack() as es2q:
            csq = es2q.enter_context(nc.sbuf_tensor("csq", [128, TC], F32))
            csq_ld = dmaC.start(lambda e: e.dma_start(out=csq[:], in_=dCq))
            wq_buf = [es2q.enter_context(
                nc.sbuf_tensor(f"wq{i}", [128, DC * 128], BF16))
                for i in range(2)]
            qts = [[es2q.enter_context(
                nc.sbuf_tensor(f"qt{j}_{i}", [64, TP], F32)) for j in range(4)]
                for i in range(2)]
            qstage = [es2q.enter_context(
                nc.sbuf_tensor(f"qst{i}", [128, TC], BF16)) for i in range(2)]
            qt_free = [0, 0]
            wq_free = [PE.n, PE.n]
            qbank_free = {0: DVE.n, 1: DVE.n, 2: DVE.n, 3: DVE.n}
            qst_free = [0, 0]
            csq_waited = {"dve": False}
            for hh in range(NH):
                wbi = hh % 2
                wld = load_w_slice(dmaWG, wq_buf[wbi], dWq, hh * 128, 128, DC,
                                   waits=[(PE, wq_free[wbi])])
                em.wait("pe", dmaWG.sem, wld)
                if hh == 0:
                    em.wait("pe", ACT, xnTm_ready)
                qb0 = 2 * (hh % 2)
                for tp in range(NTP):
                    em.wait("pe", DVE, qbank_free[qb0 + tp % 2])
                for ic in range(DC):
                    for tp in range(NTP):
                        em.op("pe", lambda e, ic=ic, tp=tp, wbi=wbi, qb0=qb0:
                              e.matmul(
                                  pb[qb0 + tp % 2][:, 0:TP],
                                  wq_buf[wbi][:, ic * 128:ic * 128 + 128],
                                  xnT_mine[:, ic * TC + tp * TP:
                                           ic * TC + tp * TP + TP],
                                  start=(ic == 0), stop=(ic == DC - 1)),
                              inc=PE)
                wq_free[wbi] = PE.n
                qsb = qstage[hh % 2]
                for tp in range(NTP):
                    bank_i = qb0 + tp % 2
                    bank = pb[bank_i]
                    t1, t2, t3, t4 = qts[tp % 2]
                    em.wait("dve", PE, PE.n)
                    if not csq_waited["dve"]:
                        em.wait("dve", dmaC.sem, csq_ld)
                        csq_waited["dve"] = True
                    em.wait("dve", DVE, qt_free[tp % 2])
                    if tp == 0:
                        em.wait("dve", dmaQS.sem, qst_free[hh % 2])
                    c0 = tp * TP
                    for dst, rlo, clo in ((t1, 0, 0), (t2, 64, 64),
                                          (t3, 0, 64), (t4, 64, 0)):
                        em.op("dve", lambda e, dst=dst, rlo=rlo, clo=clo,
                              bank=bank, c0=c0: e.tensor_tensor(
                                  dst[:, 0:TP], bank[rlo:rlo + 64, 0:TP],
                                  csq[clo:clo + 64, c0:c0 + TP], ALU.mult),
                              inc=DVE)
                    em.wait("dve", DVE, DVE.n)
                    em.op("dve", lambda e, t1=t1, t2=t2, qsb=qsb, c0=c0:
                          e.tensor_tensor(qsb[0:64, c0:c0 + TP],
                                          t1[:, 0:TP], t2[:, 0:TP],
                                          ALU.subtract), inc=DVE)
                    em.op("dve", lambda e, t3=t3, t4=t4, qsb=qsb, c0=c0:
                          e.tensor_tensor(qsb[64:128, c0:c0 + TP],
                                          t3[:, 0:TP], t4[:, 0:TP], ALU.add),
                          inc=DVE)
                    qt_free[tp % 2] = DVE.n
                    qbank_free[bank_i] = DVE.n
                qst_free[hh % 2] = dmaQS.start(
                    lambda e, qsb=qsb, hh=hh: e.dma_start(
                        out=dQrot[:, hh * TC:(hh + 1) * TC], in_=qsb[:]),
                    waits=[(DVE, DVE.n)])
            barrier()

        h_half = es2.enter_context(nc.sbuf_tensor("h_half", [128, FFH * TC],
                                                  BF16))
        ffn_done = 0
        for half in range(2):
            with contextlib.ExitStack() as es2b:
                wg_buf = [es2b.enter_context(
                    nc.sbuf_tensor(f"wg{half}_{i}", [128, DC * 128], BF16))
                    for i in range(2)]
                wu_buf = [es2b.enter_context(
                    nc.sbuf_tensor(f"wu{half}_{i}", [128, DC * 128], BF16))
                    for i in range(2)]
                silu_t = [es2b.enter_context(
                    nc.sbuf_tensor(f"silu{half}_{i}", [128, TP], F32))
                    for i in range(2)]
                wg_free = [PE.n, PE.n]
                wu_free = [PE.n, PE.n]
                silu_free = [0, 0]
                gu_bank_free = [DVE.n] * 4
                for oo in range(FFH):
                    o = half * FFH + oo
                    wbi = oo % 2
                    gld = load_w_slice(dmaWG, wg_buf[wbi], dWgT, o * 128, 128,
                                       DC, waits=[(PE, wg_free[wbi])])
                    uld = load_w_slice(dmaWU, wu_buf[wbi], dWuT, o * 128, 128,
                                       DC, waits=[(PE, wu_free[wbi])])
                    em.wait("pe", dmaWG.sem, gld)
                    em.wait("pe", dmaWU.sem, uld)
                    if oo == 0 and half == 0:
                        em.wait("pe", ACT, xnTm_ready)
                    for b in range(4):
                        em.wait("pe", DVE, gu_bank_free[b])
                    for ic in range(DC):
                        for tp in range(NTP):
                            em.op("pe", lambda e, ic=ic, tp=tp, wbi=wbi,
                                  wg_buf=wg_buf: e.matmul(
                                      pb[tp % 2][:, 0:TP],
                                      wg_buf[wbi][:, ic * 128:ic * 128 + 128],
                                      xnT_mine[:, ic * TC + tp * TP:
                                               ic * TC + tp * TP + TP],
                                      start=(ic == 0), stop=(ic == DC - 1)),
                                  inc=PE)
                        for tp in range(NTP):
                            em.op("pe", lambda e, ic=ic, tp=tp, wbi=wbi,
                                  wu_buf=wu_buf: e.matmul(
                                      pb[2 + tp % 2][:, 0:TP],
                                      wu_buf[wbi][:, ic * 128:ic * 128 + 128],
                                      xnT_mine[:, ic * TC + tp * TP:
                                               ic * TC + tp * TP + TP],
                                      start=(ic == 0), stop=(ic == DC - 1)),
                                  inc=PE)
                    gu_m = PE.n
                    wg_free[wbi] = gu_m
                    wu_free[wbi] = gu_m
                    for tp in range(NTP):
                        stl = silu_t[tp % 2]
                        em.wait("act", PE, gu_m)
                        em.wait("act", DVE, silu_free[tp % 2])
                        em.op("act", lambda e, stl=stl, tp=tp: e.activation(
                            stl[:, 0:TP], pb[tp % 2][:, 0:TP], AF.Silu,
                            bias=0.0, scale=1.0), inc=ACT)
                        em.wait("dve", ACT, ACT.n)
                        em.op("dve", lambda e, stl=stl, tp=tp, oo=oo:
                              e.tensor_tensor(
                                  h_half[:, oo * TC + tp * TP:
                                         oo * TC + tp * TP + TP],
                                  stl[:, 0:TP], pb[2 + tp % 2][:, 0:TP],
                                  ALU.mult), inc=DVE)
                        silu_free[tp % 2] = DVE.n
                        gu_bank_free[tp % 2] = DVE.n
                        gu_bank_free[2 + tp % 2] = DVE.n
                    em.wait("pe", ACT, ACT.n)
                h_ready = DVE.n
                barrier()

            with contextlib.ExitStack() as es2c:
                wd_buf = [es2c.enter_context(
                    nc.sbuf_tensor(f"wd{half}_{i}", [128, FFH * 128], BF16))
                    for i in range(2)]
                ystage = [es2c.enter_context(
                    nc.sbuf_tensor(f"yst{half}_{i}", [128, TC], F32))
                    for i in range(2)]
                fpart_t = [es2c.enter_context(
                    nc.sbuf_tensor(f"fpt{half}_{i}", [128, TC], F32))
                    for i in range(2)]
                wd_free = [PE.n, PE.n]
                ystage_free = [0, 0]
                fp_free = [0, 0]
                dn_bank_free = {4: DVE.n, 5: DVE.n}
                for od in range(DC):
                    wbi = od % 2
                    wld = dmaWD.start(
                        lambda e, wbi=wbi, od=od, half=half, wd_buf=wd_buf:
                        e.dma_start(
                            out=wd_buf[wbi][:].rearrange(
                                "p (fc c) -> p fc c", c=128),
                            in_=dWdT[half * FFH * 128:(half + 1) * FFH * 128,
                                     od * 128:od * 128 + 128].rearrange(
                                         "(fc p) c -> p fc c", p=128)),
                        waits=[(PE, wd_free[wbi])])
                    em.wait("pe", dmaWD.sem, wld)
                    if od == 0:
                        em.wait("pe", DVE, h_ready)
                    for tp in range(NTP):
                        em.wait("pe", DVE, dn_bank_free[4 + tp % 2])
                    for fc in range(FFH):
                        for tp in range(NTP):
                            em.op("pe", lambda e, fc=fc, tp=tp, wbi=wbi,
                                  wd_buf=wd_buf: e.matmul(
                                      pb[4 + tp % 2][:, 0:TP],
                                      wd_buf[wbi][:, fc * 128:fc * 128 + 128],
                                      h_half[:, fc * TC + tp * TP:
                                             fc * TC + tp * TP + TP],
                                      start=(fc == 0), stop=(fc == FFH - 1)),
                                  inc=PE)
                    dn_m = PE.n
                    wd_free[wbi] = dn_m
                    ys = ystage[od % 2]
                    em.wait("dve", PE, dn_m)
                    em.wait("dve", dmaYST.sem, ystage_free[od % 2])
                    if half == 0:
                        for tp in range(NTP):
                            em.op("dve", lambda e, ys=ys, tp=tp: e.tensor_copy(
                                ys[:, tp * TP:tp * TP + TP],
                                pb[4 + tp % 2][:, 0:TP]), inc=DVE)
                        for tp in range(NTP):
                            dn_bank_free[4 + tp % 2] = DVE.n
                        ystage_free[od % 2] = dmaYST.start(
                            lambda e, ys=ys, od=od: e.dma_start(
                                out=dFpart[od * 128:od * 128 + 128, :],
                                in_=ys[:]), waits=[(DVE, DVE.n)])
                    else:
                        fp = fpart_t[od % 2]
                        fld = dmaFPL.start(lambda e, fp=fp, od=od: e.dma_start(
                            out=fp[:], in_=dFpart[od * 128:od * 128 + 128, :]),
                            waits=[(DVE, fp_free[od % 2])])
                        em.wait("dve", dmaFPL.sem, fld)
                        for tp in range(NTP):
                            em.op("dve", lambda e, ys=ys, fp=fp, tp=tp:
                                  e.tensor_tensor(
                                      ys[:, tp * TP:tp * TP + TP],
                                      fp[:, tp * TP:tp * TP + TP],
                                      pb[4 + tp % 2][:, 0:TP], ALU.add),
                                  inc=DVE)
                        for tp in range(NTP):
                            dn_bank_free[4 + tp % 2] = DVE.n
                        fp_free[od % 2] = DVE.n
                        ystage_free[od % 2] = dmaYST.start(
                            lambda e, ys=ys, od=od: e.dma_start(
                                out=dFfn[od * 128:od * 128 + 128, :],
                                in_=ys[:]), waits=[(DVE, DVE.n)])
                if half == 1:
                    ffn_done = dmaYST.sem.n
                barrier()

    with contextlib.ExitStack() as es4:
        NTB = T // 128
        ksb2 = [es4.enter_context(nc.sbuf_tensor(f"ksb{i}", [128, T], BF16))
                for i in range(2)]
        vsb2 = [es4.enter_context(nc.sbuf_tensor(f"vsb{i}", [128, NTB * 129],
                                                 BF16)) for i in range(2)]
        NB_P = 6
        psb = es4.enter_context(nc.sbuf_tensor("psb", [128, NB_P * 512],
                                               BF16))
        attst = [es4.enter_context(nc.sbuf_tensor(f"atts{i}", [128, G * 128],
                                                  BF16)) for i in range(2)]
        rs_t = [es4.enter_context(nc.sbuf_tensor(f"rs{i}", [128, 1], F32))
                for i in range(2)]
        qrotT = es4.enter_context(nc.sbuf_tensor("qrotT", [128, NH * TC],
                                                 BF16))
        qrot_ld = dmaC.start(lambda e: e.dma_start(out=qrotT[:], in_=dQrot))

        for i in range(2):
            em.op("pool", lambda e, i=i: e.memset(
                vsb2[i][:].rearrange("p (tb e) -> p tb e", e=129)[
                    :, :, 128:129], 1.0), inc=POOL)
        ones_ready = POOL.n

        ST_B = (0, 1, 2)
        AV_B = (3, 4)
        st_bank_free = {b: 0 for b in ST_B}
        psb_slot_free = [0] * NB_P
        av_bank_free = {b: DVE.n for b in AV_B}
        exp_marks = {}
        att_free_t = [0, 0]
        kv_loaded = [0, 0]
        kv_free = [PE.n, PE.n]
        QR = DKV + SPAN

        def issue_kv_load(j):
            par = j % 2
            w0 = [(PE, kv_free[par])]
            if j == 0:
                w0.append((CC, 1))
            dmaKVL.start(lambda e, par=par, j=j: e.dma_start(
                out=ksb2[par][:].rearrange("p (r c) -> p r c", c=SPAN),
                in_=dGath.rearrange("(r q) c -> q r c", q=QR)[
                    j * 128:j * 128 + 128]), waits=w0)
            for r in range(NCORES):
                dmaKVL.start(lambda e, par=par, j=j, r=r: e.dma_start(
                    out=vsb2[par][:].rearrange("p (tb e) -> p tb e", e=129)[
                        :, r * (SPAN // 128):(r + 1) * (SPAN // 128), 0:128],
                    in_=dGath[r * QR + DKV:(r + 1) * QR,
                              j * 128:j * 128 + 128].rearrange(
                        "(jj p) c -> p jj c", p=128)))
            kv_loaded[par] = dmaKVL.sem.n

        gi_n = [0]
        avq = []
        LAG = 2

        def emit_st_grp(h, g, t, gr, ngr):
            gi = gi_n[0]
            gi_n[0] += 1
            par = h % 2
            bank_i = ST_B[gi % 3]
            bank = pb[bank_i]
            slot = gi % NB_P
            qcol = (h * G + g) * TC + t * 128
            diag = (gr >= ngr - 2)
            em.wait("pe", ACT, st_bank_free[bank_i])
            for j in range(4):
                kk = gr * 4 + j
                em.op("pe", lambda e, j=j, kk=kk, bank=bank, par=par,
                      qcol=qcol, diag=diag: e.matmul(
                          bank[:, 128 * j:128 * j + 128],
                          ksb2[par][:, 128 * kk:128 * kk + 128],
                          qrotT[:, qcol:qcol + 128],
                          start=True, stop=not diag), inc=PE)
                if diag:
                    moff = (kk - 8 * t) * 128
                    em.op("pe", lambda e, j=j, bank=bank, moff=moff:
                          e.matmul(bank[:, 128 * j:128 * j + 128],
                                   ident[:], maskm[:, moff:moff + 128],
                                   start=False, stop=True), inc=PE)
            em.wait("act", PE, PE.n)
            em.wait("act", PE, psb_slot_free[slot])
            em.op("act", lambda e, slot=slot, bank=bank: e.activation(
                psb[:, slot * 512:slot * 512 + 512], bank[:],
                AF.Exp, bias=0.0, scale=1.0), inc=ACT)
            exp_marks[gi] = ACT.n
            st_bank_free[bank_i] = ACT.n
            return gi

        def emit_av_grp(h, g, t, gr, S, gi, si):
            par = h % 2
            avb_i = AV_B[si % 2]
            avb = pb[avb_i]
            slot = gi % NB_P
            em.wait("pe", ACT, exp_marks[gi])
            if gr == 0:
                em.wait("pe", DVE, av_bank_free[avb_i])
                em.wait("pe", POOL, ones_ready)
            for j in range(4):
                kk = gr * 4 + j
                em.op("pe", lambda e, j=j, kk=kk, avb=avb, par=par,
                      slot=slot, S=S: e.matmul(
                          avb[:, 0:129],
                          psb[:, slot * 512 + 128 * j:
                              slot * 512 + 128 * j + 128],
                          vsb2[par][:, 129 * kk:129 * kk + 129],
                          start=(kk == 0), stop=(kk == S - 1)), inc=PE)
            psb_slot_free[slot] = PE.n
            if gr == S // 4 - 1:
                pqr = si % 2
                em.wait("dve", PE, PE.n)
                em.wait("dve", DVE, DVE.n)
                em.op("dve", lambda e, pqr=pqr, avb=avb: e.reciprocal(
                    rs_t[pqr][:], avb[:, 128:129]), inc=DVE)
                em.wait("dve", DVE, DVE.n)
                em.wait("dve", dmaATT.sem, att_free_t[t % 2])
                em.op("dve", lambda e, g=g, t=t, pqr=pqr, avb=avb:
                      e.tensor_scalar(
                          attst[t % 2][:, g * 128:g * 128 + 128],
                          avb[:, 0:128], rs_t[pqr][:], None, ALU.mult),
                      inc=DVE)
                av_bank_free[avb_i] = DVE.n
                if g == G - 1:
                    att_free_t[t % 2] = dmaATT.start(
                        lambda e, h=h, t=t: e.dma_start(
                            out=dAtt[t * 128:t * 128 + 128,
                                     h * G * 128:(h + 1) * G * 128],
                            in_=attst[t % 2][:]),
                        waits=[(DVE, DVE.n)])

        def pump(limit):
            while len(avq) > limit:
                item = avq.pop(0)
                emit_av_grp(*item[:7])
                if item[7]:
                    kv_free[item[0] % 2] = PE.n

        issue_kv_load(0)
        si_n = [0]
        for h in range(NKV):
            em.wait("pe", dmaKVL.sem, kv_loaded[h % 2])
            if h == 0:
                em.wait("pe", dmaC.sem, qrot_ld)
            for t in range(NQB):
                for g in range(G):
                    si = si_n[0]
                    si_n[0] += 1
                    S = 8 * t + 8
                    for gr in range(S // 4):
                        gi = emit_st_grp(h, g, t, gr, S // 4)
                        head_last = (t == NQB - 1 and g == G - 1
                                     and gr == S // 4 - 1)
                        avq.append((h, g, t, gr, S, gi, si, head_last))
                        pump(LAG)
                if t == 0 and h + 1 < NKV:
                    pump(0)
                    issue_kv_load(h + 1)
        pump(0)
        barrier()

    with contextlib.ExitStack() as es5:
        attT = es5.enter_context(nc.sbuf_tensor("attT", [128, DC * TC], BF16))
        wo_buf = [es5.enter_context(
            nc.sbuf_tensor(f"wo{i}", [128, DC * 128], BF16)) for i in range(2)]
        xr_t = [es5.enter_context(nc.sbuf_tensor(f"xr{i}", [128, TC], F32))
                for i in range(2)]
        fr_t = [es5.enter_context(nc.sbuf_tensor(f"fr{i}", [128, TC], F32))
                for i in range(2)]
        o1_t = [es5.enter_context(nc.sbuf_tensor(f"o1{i}", [128, TC], F32))
                for i in range(2)]

        att_tile = [es5.enter_context(nc.sbuf_tensor(f"at5_{i}", [128, D],
                                                     BF16)) for i in range(2)]
        at_free = [PE.n, PE.n]
        attT_ready = 0
        for tt in range(TC // 128):
            bi = tt % 2
            ld = dmaATT.start(lambda e, tt=tt, bi=bi: e.dma_start(
                out=att_tile[bi][:], in_=dAtt[tt * 128:tt * 128 + 128, :]),
                waits=[(PE, at_free[bi])])
            em.wait("pe", dmaATT.sem, ld)
            attT_ready, pem = transpose_tiles(att_tile[bi], attT, TC,
                                              tt * 128, 0, DC)
            at_free[bi] = pem
        wo_free = [PE.n, PE.n]
        obank_free = {0: DVE.n, 1: DVE.n}
        res_marks = [0, 0]
        out_marks = [0, 0]
        for od in range(DC):
            wbi = od % 2
            wld = load_w_slice(dmaWG, wo_buf[wbi], dWo, od * 128, 128, DC,
                               waits=[(PE, wo_free[wbi])])
            xr, fr, o1 = xr_t[od % 2], fr_t[od % 2], o1_t[od % 2]
            xld = dmaXR.start(lambda e, xr=xr, od=od: e.dma_start(
                out=xr[:], in_=dXqT[od * 128:od * 128 + 128, :]),
                waits=[(DVE, res_marks[od % 2])])
            fld = dmaFR.start(lambda e, fr=fr, od=od: e.dma_start(
                out=fr[:], in_=dFfn[od * 128:od * 128 + 128, :]),
                waits=([(dmaYST.sem, ffn_done), (DVE, res_marks[od % 2])]
                       if od == 0 else [(DVE, res_marks[od % 2])]))
            em.wait("pe", dmaWG.sem, wld)
            if od == 0:
                em.wait("pe", ACT, attT_ready)
            for tp in range(NTP):
                em.wait("pe", DVE, obank_free[tp % 2])
            for ic in range(DC):
                for tp in range(NTP):
                    em.op("pe", lambda e, ic=ic, tp=tp, wbi=wbi: e.matmul(
                        pb[tp % 2][:, 0:TP],
                        wo_buf[wbi][:, ic * 128:ic * 128 + 128],
                        attT[:, ic * TC + tp * TP: ic * TC + tp * TP + TP],
                        start=(ic == 0), stop=(ic == DC - 1)), inc=PE)
            wo_free[wbi] = PE.n
            em.wait("dve", PE, PE.n)
            em.wait("dve", dmaXR.sem, xld)
            em.wait("dve", dmaFR.sem, fld)
            em.wait("dve", dmaOUT.sem, out_marks[od % 2])
            for tp in range(NTP):
                em.op("dve", lambda e, o1=o1, xr=xr, tp=tp: e.tensor_tensor(
                    o1[:, tp * TP:tp * TP + TP], xr[:, tp * TP:tp * TP + TP],
                    pb[tp % 2][:, 0:TP], ALU.add), inc=DVE)
                obank_free[tp % 2] = DVE.n
            em.wait("dve", DVE, DVE.n)
            em.op("dve", lambda e, o1=o1, fr=fr: e.tensor_tensor(
                o1[:], o1[:], fr[:], ALU.add), inc=DVE)
            res_marks[od % 2] = DVE.n
            out_marks[od % 2] = dmaOUT.start(
                lambda e, o1=o1, od=od: e.dma_start(
                    out=dOut[od * 128:od * 128 + 128, :], in_=o1[:]),
                waits=[(DVE, DVE.n)])
        em.wait("sp", dmaOUT.sem, dmaOUT.sem.n)

    em.flush(nc)
    es.close()
    return nc



def _host_inputs(cfg, core, X, Wq, Wk, Wv, Wo, Wg, Wu, Wd):
    T = cfg.T
    f32 = np.float32
    bf16 = ml_dtypes.bfloat16
    Xq = np.ascontiguousarray(X[core::NCORES])
    XqT = np.ascontiguousarray(Xq.T)
    Xkv = np.ascontiguousarray(X[core * SPAN:(core + 1) * SPAN])

    half = DH // 2
    inv = 1.0 / (10000.0 ** (np.arange(half, dtype=np.float64) / half))
    pos_blk = np.arange(core * SPAN, (core + 1) * SPAN, dtype=np.float64)
    ang = pos_blk[None, :] * inv[:, None]
    cs_kv = np.concatenate([np.cos(ang), np.sin(ang)], axis=0).astype(f32)
    pos_q = np.arange(T, dtype=np.float64)[core::NCORES]
    angq = pos_q[None, :] * inv[:, None]
    s = 1.0 / np.sqrt(np.float64(DH))
    cs_q = np.concatenate([np.cos(angq) * s, np.sin(angq) * s],
                          axis=0).astype(f32)

    i_ = np.arange(QBLK)[:, None]
    rj = np.arange(SPAN)[None, :]
    mp = (128 * (rj // 128) + i_) <= (NCORES * (rj % 128) + core)
    maskm = np.where(mp, 0.0, -30000.0).astype(f32).astype(bf16)
    ident = np.eye(128, dtype=f32).astype(bf16)

    return {
        "Xkv": Xkv, "Xq": Xq, "XqT": XqT,
        "Wq": Wq, "Wk": Wk, "Wv": Wv, "Wo": Wo,
        "WgT": np.ascontiguousarray(Wg.T),
        "WuT": np.ascontiguousarray(Wu.T),
        "WdT": np.ascontiguousarray(Wd.T),
        "cs_kv": cs_kv, "cs_q": cs_q, "maskm": maskm, "ident": ident,
    }


def run(cfg, X, Wq, Wk, Wv, Wo, Wg, Wu, Wd, trace=False):
    nc = build_program(cfg)
    in_maps = [_host_inputs(cfg, c, X, Wq, Wk, Wv, Wo, Wg, Wu, Wd)
               for c in range(NCORES)]
    res = run_bass_kernel_spmd(nc, in_maps, core_ids=list(range(NCORES)),
                               trace=trace)
    out = np.empty((cfg.T, cfg.D), np.float32)
    for c in range(NCORES):
        out[c::NCORES, :] = res.results[c]["outT"].T
    return out, res


def kernel(L0_attn_X, L0_attn_Wq, L0_attn_Wk, L0_attn_Wv, L0_attn_Wo,
           L0_ffn_W_gate, L0_ffn_W_up, L0_ffn_W_down):
    cfg = Cfg()
    X = np.ascontiguousarray(np.asarray(L0_attn_X, np.float32).reshape(
        cfg.T, cfg.D))
    out, _ = run(cfg, X,
                 np.asarray(L0_attn_Wq, np.float32),
                 np.asarray(L0_attn_Wk, np.float32),
                 np.asarray(L0_attn_Wv, np.float32),
                 np.asarray(L0_attn_Wo, np.float32),
                 np.asarray(L0_ffn_W_gate, np.float32),
                 np.asarray(L0_ffn_W_up, np.float32),
                 np.asarray(L0_ffn_W_down, np.float32))
    return out.reshape(1, cfg.T, cfg.D)
